# version 09
# baseline (speedup 1.0000x reference)
import numpy as np
import concourse.bass as bass
import concourse.mybir as mybir
from concourse.bass_utils import run_bass_kernel_spmd
from contextlib import ExitStack
from collections import deque

F32 = mybir.dt.float32
BF16 = mybir.dt.bfloat16
U32 = mybir.dt.uint32
ALU = mybir.AluOpType
AF = mybir.ActivationFunctionType
AX = mybir.AxisListType

D = 2048
SEQ = 4096
OWN = 2048
NT = 4
NDC = 16
INTER = 8192
MOE = 1024
NI = 72
NI_SH = 64
ND5 = 4
HLOC = 8
E = 128
SCALE = float(1.0 / np.sqrt(128.0))
RSQRT_MAGIC = 0x5F3759DF

BUILD = "full"
DEBUG_OUT = "none"


class Sem:
    __slots__ = ("h", "n")

    def __init__(self, h):
        self.h = h
        self.n = 0


class Prog:
    def __init__(self, nc, stack):
        self.nc = nc
        self.ops = {k: [] for k in ("sync", "gpsimd", "scalar", "vector", "tensor")}
        self.stack = stack
        self.sems = {}
        self.scope = "init"

    def sem(self, name):
        s = self.sems.get(name)
        if s is None:
            s = Sem(self.stack.enter_context(self.nc.semaphore(name)))
            self.sems[name] = s
        return s

    def op(self, eng, fn, waits=(), inc=None, amt=1, incs=(), bump=()):
        wl = []
        for w in waits:
            s, v = (w if isinstance(w, tuple) else (w, w.n))
            if v > 0:
                wl.append((s.h, v))
        il = list(incs)
        if inc is not None:
            il.append((inc, amt))
        self.ops[eng].append((fn, wl, [(s.h, a) for s, a in il], self.scope))
        for s, a in il:
            s.n += a
        for s, a in bump:
            s.n += a
        if inc is not None:
            return (inc, inc.n)
        return None

    def dve_chain(self, fns, waits=()):
        sd = self.sem("s_dve")
        for i, fn in enumerate(fns):
            self.op("vector", fn,
                    waits=(list(waits) + [(sd, sd.n)]) if i == 0 else [(sd, sd.n)],
                    inc=sd)

    def emit(self):
        block = self.stack.enter_context(self.nc.Block())
        for engname in ("sync", "gpsimd", "scalar", "vector", "tensor"):
            def mk(engname):
                wm = {}

                def body(eng):
                    cur = None
                    for fn, wl, il, sc in self.ops[engname]:
                        if sc != cur:
                            if cur is not None:
                                self.nc.pop_named_scope(cur)
                            self.nc.push_named_scope(sc)
                            cur = sc
                        for h, v in wl:
                            if wm.get(h, 0) < v:
                                eng.wait_ge(h, v)
                                wm[h] = v
                        ins = fn(eng)
                        for h, a in il:
                            ins = ins.then_inc(h, a)
                    if cur is not None:
                        self.nc.pop_named_scope(cur)
                return body
            getattr(block, engname)(mk(engname))


def build_program(build=BUILD, debug_out=DEBUG_OUT):
    nc = bass.Bass("TRN2", target_bir_lowering=False)

    ap_x = nc.dram_tensor("x", [OWN, D], F32, kind="ExternalInput").ap()
    ap_gate = nc.dram_tensor("gate_t", [32, 16, 128, 256], F32, kind="ExternalInput").ap()
    ap_up = nc.dram_tensor("up_t", [32, 16, 128, 256], F32, kind="ExternalInput").ap()
    ap_down = nc.dram_tensor("down_t", [4, 64, 128, 512], F32, kind="ExternalInput").ap()
    ap_rgate = nc.dram_tensor("rgate_t", [4, 16, 128, 256], F32, kind="ExternalInput").ap()
    ap_rup = nc.dram_tensor("rup_t", [4, 16, 128, 256], F32, kind="ExternalInput").ap()
    ap_rdown = nc.dram_tensor("rdown_t", [4, 8, 128, 512], F32, kind="ExternalInput").ap()
    ap_wr = nc.dram_tensor("wr_t", [16, 128, 8], F32, kind="ExternalInput").ap()
    ap_wq = nc.dram_tensor("wq_t", [16, 128, HLOC * 128], F32, kind="ExternalInput").ap()
    ap_wk = nc.dram_tensor("wk_t", [16, 128, HLOC * 128], F32, kind="ExternalInput").ap()
    ap_wv = nc.dram_tensor("wv_t", [16, 128, HLOC * 128], F32, kind="ExternalInput").ap()
    ap_wo = nc.dram_tensor("wo_t", [16, 128, D], F32, kind="ExternalInput").ap()
    ap_magic = nc.dram_tensor("magic", [1, 64], U32, kind="ExternalInput").ap()
    ap_y = nc.dram_tensor("y", [OWN, D], F32, kind="ExternalOutput").ap()
    ap_pid = nc.partition_id_tensor.ap()

    ap_res1 = nc.dram_tensor("res1", [OWN, D], F32, kind="Internal").ap()
    ap_res2 = nc.dram_tensor("res2", [OWN, D], F32, kind="Internal").ap()
    ap_xln2t = nc.dram_tensor("xln2t", [16, 128, OWN], BF16, kind="Internal").ap()
    ap_cgate = nc.dram_tensor("cgate", [32, 128, 4096], BF16, kind="Internal").ap()
    ap_cup = nc.dram_tensor("cup", [32, 128, 4096], BF16, kind="Internal").ap()
    ap_cdown = nc.dram_tensor("cdown", [4, 9, 128, 4096], BF16, kind="Internal").ap()
    ap_crgate = nc.dram_tensor("crgate", [4, 128, 4096], BF16, kind="Internal").ap()
    ap_crup = nc.dram_tensor("crup", [4, 128, 4096], BF16, kind="Internal").ap()
    ap_crdown = nc.dram_tensor("crdown", [4, 1, 128, 4096], BF16, kind="Internal").ap()
    ap_shx = nc.dram_tensor("shx", [16, 128, SEQ], BF16, kind="Internal", addr_space="Shared").ap()
    ap_shz = nc.dram_tensor("shz", [16, 128, SEQ], BF16, kind="Internal", addr_space="Shared").ap()
    ap_shf = nc.dram_tensor("shf", [8, 64], U32, kind="Internal", addr_space="Shared").ap()

    stack = ExitStack()
    with stack:
        e = stack.enter_context
        P = Prog(nc, stack)

        ident = e(nc.sbuf_tensor("ident", [128, 128], BF16))
        wr_bf = e(nc.sbuf_tensor("wr_bf", [128, 128], BF16))
        sb_pid = e(nc.sbuf_tensor("sb_pid", [1, 1], U32))
        sb_magic = e(nc.sbuf_tensor("sb_magic", [1, 64], U32))
        sb_flag = e(nc.sbuf_tensor("sb_flag", [1, 1], U32))

        xch = e(nc.sbuf_tensor("xch", [128, 4 * D], F32))
        sq = e(nc.sbuf_tensor("sq", [128, 1024], F32))
        sqd = e(nc.sbuf_tensor("sqd", [128, 1024], F32))
        stats = e(nc.sbuf_tensor("stats", [128, 48], F32))
        lgt = e(nc.sbuf_tensor("lgt", [128, 64], F32))
        xlnt = e(nc.sbuf_tensor("xlnt", [128, NDC * 512], BF16))
        xtm = e(nc.sbuf_tensor("xtm", [128, D], BF16))
        hT = e(nc.sbuf_tensor("hT", [128, NI * 512], BF16))
        wf32 = e(nc.sbuf_tensor("wf32", [128, 4096], F32))
        wbf = [e(nc.sbuf_tensor(f"wbf{i}", [128, 4096], BF16)) for i in range(4)]
        satt = e(nc.sbuf_tensor("satt", [128, 4096], F32))
        sg_sb = [e(nc.sbuf_tensor(f"sg{i}", [128, 512], F32)) for i in range(2)]
        tmp5 = [e(nc.sbuf_tensor(f"tmp5_{i}", [128, 512], F32)) for i in range(2)]

        pb = [e(nc.psum_tensor(f"pb{i}", [128, 512], F32)) for i in range(8)]
        pbb4 = pb[4].ap().bitcast(BF16)

        s_in = P.sem("s_in")
        s_w = P.sem("s_w")
        s_cvt = P.sem("s_cvt")
        s_pe = P.sem("s_pe")
        s_act = P.sem("s_act")
        s_dve = P.sem("s_dve")
        s_pool = P.sem("s_pool")
        s_st = P.sem("s_st")
        s_cache = P.sem("s_cache")
        s_ld2 = P.sem("s_ld2")
        s_y = P.sem("s_y")
        s_x = P.sem("s_x")
        spoll = P.sem("s_poll")

        P.op("gpsimd", lambda g: g.dma_start(out=sb_pid[:, :], in_=ap_pid[0:1, 0:1]), inc=s_st, amt=16)
        P.op("gpsimd", lambda g: g.dma_start(out=sb_magic[:, :], in_=ap_magic[:, :]), inc=s_st, amt=16)
        P.op("gpsimd", lambda g: g.memset(ident[:, :], 1.0), inc=s_pool)
        P.op("gpsimd", lambda g: g.affine_select(out=ident[:, :], in_=ident[:, :],
                                                 pattern=[[1, 128]], compare_op=ALU.is_equal,
                                                 fill=0.0, base=0, channel_multiplier=-1),
             waits=[s_pool], inc=s_pool)
        P.op("sync", lambda sy: sy.dma_start(
            out=wf32[:, 0:128].rearrange("p (a b) -> p a b", a=16),
            in_=ap_wr.rearrange("a p b -> p a b")), inc=s_w, amt=16)
        P.op("vector", lambda v: v.tensor_copy(out=wr_bf[:, :], in_=wf32[:, 0:128]),
             waits=[s_w], inc=s_cvt)

        regs = {}

        def setup_regs(g):
            for r in ("preg", "rmagic", "rflag", "rcond", "rwait", "rfill"):
                regs[r] = g.alloc_register(r)
            g.reg_mov(regs["rfill"], -1e9)
            g.reg_load(regs["preg"], sb_pid[0:1, 0:1])
            g.reg_alu(regs["preg"], regs["preg"], 1, ALU.bitwise_and)
            g.reg_load(regs["rmagic"], sb_magic[0:1, 0:1])
            return g.reg_mov(regs["rwait"], 0)

        P.op("gpsimd", setup_regs, waits=[(s_st, 32)], inc=s_pool)

        def ln_stats_chunk(xsl, col, dep):
            P.op("vector", lambda v: v.tensor_scalar(
                sqd[:, :], xsl[:, 0:1024], 1.0, 0.0, ALU.mult, ALU.add,
                accum_out=stats[:, col:col + 1]), waits=[dep, (s_dve, s_dve.n)], inc=s_dve)
            P.op("vector", lambda v: v.tensor_scalar(
                sqd[:, :], xsl[:, 1024:2048], 1.0, 0.0, ALU.mult, ALU.add,
                accum_out=stats[:, 4 + col:5 + col]), waits=[(s_dve, s_dve.n)], inc=s_dve)
            P.op("scalar", lambda s: s.activation(
                sq[:, :], xsl[:, 0:1024], AF.Square,
                accum_out=stats[:, 8 + col:9 + col]), waits=[dep, (s_act, s_act.n)], inc=s_act)
            P.op("scalar", lambda s: s.activation(
                sq[:, :], xsl[:, 1024:2048], AF.Square,
                accum_out=stats[:, 12 + col:13 + col]), waits=[(s_act, s_act.n)], inc=s_act)

        def ln_batch_rstd():
            st = stats
            P.dve_chain([
                lambda v: v.tensor_tensor(st[:, 24:28], st[:, 0:4], st[:, 4:8], ALU.add),
                lambda v: v.tensor_tensor(st[:, 28:32], st[:, 8:12], st[:, 12:16], ALU.add),
                lambda v: v.tensor_scalar(st[:, 16:20], st[:, 24:28], 1.0 / D, None, ALU.mult),
                lambda v: v.tensor_scalar(st[:, 28:32], st[:, 28:32], 1.0 / D, None, ALU.mult),
                lambda v: v.tensor_tensor(st[:, 24:28], st[:, 16:20], st[:, 16:20], ALU.mult),
                lambda v: v.tensor_tensor(st[:, 28:32], st[:, 28:32], st[:, 24:28], ALU.subtract),
                lambda v: v.tensor_scalar(st[:, 28:32], st[:, 28:32], 1.0, 1e-5, ALU.mult, ALU.add),
            ], waits=[(s_act, s_act.n)])
            P.op("scalar", lambda s: s.activation(st[:, 36:40], st[:, 28:32], AF.Ln),
                 waits=[(s_dve, s_dve.n)], inc=s_act)
            P.op("scalar", lambda s: s.activation(st[:, 32:36], st[:, 36:40], AF.Exp,
                                                  scale=-0.5),
                 waits=[(s_act, s_act.n)], inc=s_act)

        def transpose_chunk(srcap, dst_sb, colbase, dep_dve):
            dstv = dst_sb.ap().rearrange("p (dc f) -> p dc f", dc=NDC)
            srcv = pbb4.rearrange("p (g f) -> p g f", g=8)[:, 0:4, :]
            for grp in range(4):
                for k in range(4):
                    dc = grp * 4 + k
                    P.op("tensor", lambda t, dc=dc, k=k: t.transpose(
                        pbb4[:, k * 128:(k + 1) * 128],
                        srcap[:, dc * 128:(dc + 1) * 128], ident[:, :]),
                        waits=([(s_pool, 2), dep_dve, (s_act, s_act.n)]
                               if k == 0 else []),
                        inc=(s_pe if k == 3 else None))
                P.op("scalar", lambda s, grp=grp: s.activation(
                    dstv[:, grp * 4:(grp + 1) * 4, colbase:colbase + 128],
                    srcv, AF.Copy),
                    waits=[(s_pe, s_pe.n), (s_st, s_st.n)], inc=s_act)

        def emit_ffn(phase):
            first = (phase == 1)
            src_res = ap_x if first else ap_res2
            s_x1 = P.sem(f"sx1_{phase}")
            s_hw = P.sem(f"shw_{phase}")
            s_hr = P.sem(f"shr_{phase}")
            sd_slot = [P.sem(f"sdslot{i}_{phase}") for i in range(2)]
            sd_rdy = [P.sem(f"sdrdy{i}_{phase}") for i in range(2)]
            sd_bank = [P.sem(f"sdbank{i}_{phase}") for i in range(8)]
            sw_slot = [P.sem(f"swslot{i}_{phase}") for i in range(4)]
            sg_bank = [P.sem(f"sgbank{i}_{phase}") for i in range(2)]
            su_bank = [P.sem(f"subank{i}_{phase}") for i in range(2)]
            sw_rdy = [P.sem(f"swrdy{i}_{phase}") for i in range(4)]

            fstage = [wf32.ap()[:, :], satt.ap()[:, :]]
            fqueue = ["sync", "scalar"]
            fqsem = [s_w, s_ld2]
            fst = {"i": 0, "free": [None, None]}

            for t in range(NT):
                P.scope = f"ffn{phase}_t{t}"
                cachepass = first and t == 0
                CACHE_GU, CACHE_DN = True, True
                for sc in range(4):
                    row0 = t * 512 + sc * 128
                    xsl = xch.ap()[:, sc * D:(sc + 1) * D]
                    P.op("gpsimd", lambda g, r=row0, d=xsl: g.dma_start(
                        out=d, in_=src_res[r:r + 128, :]),
                        waits=[(s_x1, s_x1.n)], inc=s_in, amt=16)
                    dep = (s_in, s_in.n)
                    if first:
                        ln_stats_chunk(xsl, sc, dep)
                if first:
                    ln_batch_rstd()
                    for sc in range(4):
                        xsl = xch.ap()[:, sc * D:(sc + 1) * D]
                        P.op("vector", lambda v, sc=sc, xsl=xsl: v.tensor_scalar(
                            xtm[:, :], xsl, stats[:, 16 + sc:17 + sc],
                            stats[:, 32 + sc:33 + sc], ALU.subtract, ALU.mult),
                            waits=[(s_dve, s_dve.n), (s_act, s_act.n), (s_pe, s_pe.n)],
                            inc=s_dve)
                        transpose_chunk(xtm.ap(), xlnt, sc * 128, (s_dve, s_dve.n))
                else:
                    for dc in range(NDC):
                        P.op("scalar", lambda s, dc=dc, t=t: s.dma_start(
                            out=xlnt[:, dc * 512:(dc + 1) * 512],
                            in_=ap_xln2t[dc, :, t * 512:(t + 1) * 512]),
                            waits=[(s_st, s_st.n), (s_pe, s_pe.n)],
                            inc=s_ld2, amt=16)
                xlnt_dep = (s_act, s_act.n) if first else (s_ld2, s_ld2.n)
                if debug_out == "ln" and first and t == 0:
                    P.op("gpsimd", lambda g: g.dma_start(
                        out=ap_y[0:128, 0:48], in_=stats[:, :]),
                        waits=[(s_dve, s_dve.n), (s_act, s_act.n)], inc=s_y, amt=16)
                    P.op("scalar", lambda s: s.activation(
                        sq[:, :], xlnt[:, 0:1024], AF.Copy),
                        waits=[(s_act, s_act.n), (s_dve, s_dve.n)], inc=s_act)
                    P.op("gpsimd", lambda g: g.dma_start(
                        out=ap_y[128:256, 0:1024], in_=sq[:, :]),
                        waits=[(s_act, s_act.n)], inc=s_y, amt=16)
                    P.op("gpsimd", lambda g: g.dma_start(
                        out=ap_y[256:384, :], in_=xch[:, 0:2048]),
                        waits=[(s_in, s_in.n)], inc=s_y, amt=16)
                for sc in range(4):
                    for dc in range(NDC):
                        P.op("tensor", lambda t_, sc=sc, dc=dc: t_.matmul(
                            pb[5][:, 0:8],
                            xlnt[:, dc * 512 + sc * 128: dc * 512 + sc * 128 + 128],
                            wr_bf[:, dc * 8:(dc + 1) * 8],
                            start=(dc == 0), stop=(dc == NDC - 1)),
                            waits=([xlnt_dep, (s_cvt, s_cvt.n), (s_dve, s_dve.n)]
                                   if dc == 0 else []),
                            inc=(s_pe if dc == NDC - 1 else None))
                    P.op("vector", lambda v, sc=sc: v.tensor_copy(
                        out=lgt[:, sc * 8:(sc + 1) * 8], in_=pb[5][:, 0:8]),
                        waits=[(s_pe, s_pe.n)], inc=s_dve)

                fns = []
                for sc in range(4):
                    fns.append(lambda v, sc=sc: v.tensor_reduce(
                        lgt[:, 32 + sc:33 + sc], lgt[:, sc * 8:(sc + 1) * 8], AX.X, ALU.max))
                for sc in range(4):
                    fns.append(lambda v, sc=sc: v.tensor_scalar(
                        lgt[:, 48:56], lgt[:, sc * 8:(sc + 1) * 8],
                        lgt[:, 32 + sc:33 + sc], None, ALU.is_ge))
                    fns.append(lambda v, sc=sc: v.scalar_tensor_tensor(
                        lgt[:, 48:56], lgt[:, 48:56], -2000.0,
                        lgt[:, sc * 8:(sc + 1) * 8], ALU.mult, ALU.add))
                    fns.append(lambda v, sc=sc: v.tensor_reduce(
                        lgt[:, 36 + sc:37 + sc], lgt[:, 48:56], AX.X, ALU.max))
                fns.append(lambda v: v.tensor_tensor(
                    lgt[:, 32:36], lgt[:, 32:36], lgt[:, 36:40], ALU.subtract))
                fns.append(lambda v: v.tensor_scalar(
                    lgt[:, 36:40], lgt[:, 32:36], -1.0, None, ALU.mult))
                P.dve_chain(fns)
                P.op("scalar", lambda s: s.activation(lgt[:, 48:56], lgt[:, 32:40],
                                                      AF.Tanh, scale=0.5),
                     waits=[(s_dve, s_dve.n)], inc=s_act)
                P.dve_chain([
                    lambda v: v.tensor_tensor(lgt[:, 56:60], lgt[:, 48:52],
                                              lgt[:, 52:56], ALU.add),
                    lambda v: v.tensor_scalar(lgt[:, 60:64], lgt[:, 56:60], 0.25, 0.5,
                                              ALU.mult, ALU.add),
                ], waits=[(s_act, s_act.n)])
                comb_dep = (s_dve, s_dve.n)

                if debug_out == "t1" and first and t == 1:
                    P.op("gpsimd", lambda g: g.dma_start(
                        out=ap_y[1024:1152, 0:512],
                        in_=xlnt.ap().bitcast(F32)[:, 0:512]),
                        waits=[(s_act, s_act.n), (s_dve, s_dve.n)], inc=s_y, amt=16)
                blocks = ([(ap_gate, ap_cgate, ap_up, ap_cup, i) for i in range(32)] +
                          [(ap_rgate, ap_crgate, ap_rup, ap_crup, i) for i in range(4)])

                def load_block(bi):
                    agf, agc, auf, auc, i256 = blocks[bi]
                    gslot, uslot = bi % 2, 2 + bi % 2
                    for (af32, acache, slot) in ((agf, agc, gslot), (auf, auc, uslot)):
                        dstb = wbf[slot]
                        if cachepass:
                            j = fst["i"] % 2
                            fst["i"] += 1
                            wl = ([fst["free"][j]] if fst["free"][j] is not None
                                  else ([(s_cvt, s_cvt.n)] if j == 0 else []))
                            ld = P.op(fqueue[j], lambda e, a=af32, i=i256, j=j: e.dma_start(
                                out=fstage[j].rearrange("p (dc ic) -> p dc ic", dc=16),
                                in_=a[i].rearrange("dc p ic -> p dc ic")),
                                waits=wl, inc=fqsem[j], amt=16)
                            cvt = P.op("vector", lambda v, d=dstb, j=j: v.tensor_copy(
                                out=d[:, :], in_=fstage[j]),
                                waits=[ld, (s_pe, s_pe.n), (s_cache, s_cache.n)],
                                inc=sw_rdy[slot])
                            fst["free"][j] = cvt
                            P.op("gpsimd", lambda g, a=acache, i=i256, d=dstb: g.dma_start(
                                out=a[i], in_=d[:, :]),
                                waits=[(sw_rdy[slot], sw_rdy[slot].n)], inc=s_cache, amt=16)
                        else:
                            P.op("sync", lambda sy, a=acache, i=i256, d=dstb: sy.dma_start(
                                out=d[:, :], in_=a[i]),
                                waits=[(s_pe, s_pe.n), (s_cache, s_cache.n)],
                                inc=sw_rdy[slot], amt=16)

                def mm_block(bi):
                    agf, agc, auf, auc, i256 = blocks[bi]
                    gslot, uslot = bi % 2, 2 + bi % 2
                    wvg = (sw_rdy[gslot], sw_rdy[gslot].n)
                    wvu = (sw_rdy[uslot], sw_rdy[uslot].n)
                    for ii in range(2):
                        ic = bi * 2 + ii
                        gb, ub = pb[ic % 2], pb[2 + ic % 2]
                        for dc in range(NDC):
                            P.op("tensor", lambda t_, ii=ii, dc=dc, gb=gb, s=gslot: t_.matmul(
                                gb[:, :],
                                wbf[s][:, dc * 256 + ii * 128: dc * 256 + ii * 128 + 128],
                                xlnt[:, dc * 512:(dc + 1) * 512],
                                start=(dc == 0), stop=(dc == NDC - 1)),
                                waits=([wvg, (sg_bank[ic % 2], sg_bank[ic % 2].n), xlnt_dep]
                                       if dc == 0 else []),
                                inc=None)
                        for dc in range(NDC):
                            last = dc == NDC - 1
                            P.op("tensor", lambda t_, ii=ii, dc=dc, ub=ub, s=uslot: t_.matmul(
                                ub[:, :],
                                wbf[s][:, dc * 256 + ii * 128: dc * 256 + ii * 128 + 128],
                                xlnt[:, dc * 512:(dc + 1) * 512],
                                start=(dc == 0), stop=(dc == NDC - 1)),
                                waits=([wvu, (su_bank[ic % 2], su_bank[ic % 2].n)]
                                       if dc == 0 else []),
                                inc=(s_pe if last else None))
                        sgb = sg_sb[ic % 2]
                        P.op("scalar", lambda s, gb=gb, sgb=sgb: s.activation(
                            sgb[:, :], gb[:, :], AF.Silu),
                            waits=[(s_pe, s_pe.n), (s_hw, s_hw.n)],
                            inc=sg_bank[ic % 2])
                        P.op("vector", lambda v, sgb=sgb, ub=ub, ic=ic: v.scalar_tensor_tensor(
                            hT[:, ic * 512:(ic + 1) * 512], sgb[:, :], 1.0, ub[:, :],
                            ALU.mult, ALU.mult),
                            waits=[(sg_bank[ic % 2], sg_bank[ic % 2].n), (s_hr, s_hr.n)],
                            inc=s_hw)
                        P.op("vector", lambda v: v.engine_nop(), inc=su_bank[ic % 2])

                load_block(0)
                load_block(1)
                for bi in range(len(blocks)):
                    mm_block(bi)
                    if bi + 2 < len(blocks):
                        load_block(bi + 2)
                dep_hw = (s_hw, s_hw.n)

                def dn_load(d5, si, slot):
                    routed = si == 8
                    adf, adc = (ap_rdown, ap_crdown) if routed else (ap_down, ap_cdown)
                    icb = 0 if routed else si
                    dstb = wbf[slot]
                    if cachepass:
                        j = fst["i"] % 2
                        fst["i"] += 1
                        wl = ([fst["free"][j]] if fst["free"][j] is not None
                              else ([(s_cvt, s_cvt.n)] if j == 0 else []))
                        ld = P.op(fqueue[j], lambda e, a=adf, d5=d5, icb=icb, routed=routed,
                                  j=j: e.dma_start(
                            out=fstage[j].rearrange("p (ic d) -> p ic d", ic=8),
                            in_=(a[d5, 0:8] if routed else a[d5, icb * 8:(icb + 1) * 8])
                            .rearrange("ic p dcol -> p ic dcol")),
                            waits=wl, inc=fqsem[j], amt=16)
                        cvt = P.op("vector", lambda v, d=dstb, j=j: v.tensor_copy(
                            out=d[:, :], in_=fstage[j]),
                            waits=[ld, (s_pe, s_pe.n), (s_cache, s_cache.n)],
                            inc=sd_rdy[slot % 2])
                        fst["free"][j] = cvt
                        P.op("gpsimd", lambda g, a=adc, d5=d5, icb=icb, routed=routed, d=dstb:
                             g.dma_start(out=a[d5, 0 if routed else icb], in_=d[:, :]),
                             waits=[(sd_rdy[slot % 2], sd_rdy[slot % 2].n)],
                             inc=s_cache, amt=16)
                    else:
                        P.op("sync", lambda sy, a=adc, d5=d5, icb=icb, routed=routed, d=dstb:
                             sy.dma_start(out=d[:, :], in_=a[d5, 0 if routed else icb]),
                             waits=[(s_pe, s_pe.n), (s_cache, s_cache.n)],
                             inc=sd_rdy[slot % 2], amt=16)

                def dn_mm(d5, si, slot):
                    routed = si == 8
                    dstb = wbf[slot]
                    wv = (sd_rdy[slot % 2], sd_rdy[slot % 2].n)
                    for ii in range(8):
                        ic = (NI_SH + ii) if routed else si * 8 + ii
                        for sc in range(4):
                            bidx = (4 + sc) if routed else sc
                            laststop = (ic == NI_SH - 1 or ic == NI - 1)
                            P.op("tensor", lambda t_, d=dstb, ii=ii, sc=sc, ic=ic, b=bidx: t_.matmul(
                                pb[b][:, :],
                                hT[:, ic * 512 + sc * 128: ic * 512 + sc * 128 + 128],
                                d[:, ii * 512:(ii + 1) * 512],
                                start=(ic == 0 or ic == NI_SH),
                                stop=(ic == NI_SH - 1 or ic == NI - 1)),
                                waits=([wv, dep_hw,
                                        (sd_bank[bidx], sd_bank[bidx].n)]
                                       if ii == 0 else []),
                                inc=(s_pe if sc == 3 and (ii == 7 or laststop) else None))

                for d5 in range(ND5):
                    if d5 == 0:
                        dn_load(0, 0, 0)
                        dn_load(0, 1, 1)
                    for si in range(9):
                        q = d5 * 9 + si
                        dn_mm(d5, si, q % 2)
                        nq = q + 2
                        if nq < ND5 * 9:
                            dn_load(nq // 9, nq % 9, nq % 2)
                    for sc in range(4):
                        xsl = xch.ap()[:, sc * D + d5 * 512: sc * D + (d5 + 1) * 512]
                        t5 = tmp5[sc % 2]
                        P.op("vector", lambda v, sc=sc, t5=t5, xsl=xsl: v.scalar_tensor_tensor(
                            t5[:, :], pb[4 + sc][:, :], lgt[:, 60 + sc:61 + sc], xsl,
                            ALU.mult, ALU.add),
                            waits=[(s_pe, s_pe.n), comb_dep], inc=sd_bank[4 + sc])
                        P.op("vector", lambda v, sc=sc, t5=t5, xsl=xsl: v.scalar_tensor_tensor(
                            xsl, pb[sc][:, :], 0.5, t5[:, :], ALU.mult, ALU.add),
                            waits=[(sd_bank[4 + sc], sd_bank[4 + sc].n)],
                            inc=sd_bank[sc])
                if True:
                    pass
                if debug_out == "t0" and first and t == 0:
                    P.op("scalar", lambda s: s.activation(
                        sq[:, 0:512], pb[0][:, :], AF.Copy),
                        waits=[(s_pe, s_pe.n), (sd_bank[0], sd_bank[0].n)], inc=s_act)
                    P.op("scalar", lambda s: s.activation(
                        sq[:, 512:1024], pb[4][:, :], AF.Copy),
                        waits=[(s_act, s_act.n)], inc=s_act)
                    P.op("gpsimd", lambda g: g.dma_start(
                        out=ap_y[1408:1536, 0:1024], in_=sq[:, 0:1024]),
                        waits=[(s_act, s_act.n)], inc=s_y, amt=16)
                P.op("vector", lambda v: v.engine_nop(),
                     waits=[(sd_bank[i], sd_bank[i].n) for i in range(8)],
                     inc=s_dve)
                P.op("tensor", lambda t_: t_.nop(), incs=[(s_hr, NI)])

                for sc in range(4):
                    row0 = t * 512 + sc * 128
                    xsl = xch.ap()[:, sc * D:(sc + 1) * D]
                    P.op("gpsimd", lambda g, r=row0, xsl=xsl: g.dma_start(
                        out=(ap_res1 if first else ap_y)[r:r + 128, :], in_=xsl),
                        waits=[(s_dve, s_dve.n)],
                        inc=(s_st if first else s_y), amt=16)
                    if debug_out == "x1" and first:
                        P.op("gpsimd", lambda g, r=row0, xsl=xsl: g.dma_start(
                            out=ap_y[r:r + 128, :], in_=xsl),
                            waits=[(s_dve, s_dve.n)], inc=s_y, amt=16)
                    if first and build == "full":
                        ln_stats_chunk(xsl, sc, (s_dve, s_dve.n))
                if first and build == "full":
                    ln_batch_rstd()
                    for sc in range(4):
                        xsl = xch.ap()[:, sc * D:(sc + 1) * D]
                        P.op("vector", lambda v, sc=sc, xsl=xsl: v.tensor_scalar(
                            xtm[:, :], xsl, stats[:, 16 + sc:17 + sc],
                            stats[:, 32 + sc:33 + sc], ALU.subtract, ALU.mult),
                            waits=[(s_dve, s_dve.n), (s_act, s_act.n), (s_pe, s_pe.n)],
                            inc=s_dve)
                        transpose_chunk(xtm.ap(), xlnt, sc * 128, (s_dve, s_dve.n))

                    def ship(g, t=t):
                        with g.If_eq(regs["preg"], 0):
                            for dc in range(NDC):
                                g.dma_start(out=ap_shx[dc, :, t * 512:(t + 1) * 512],
                                            in_=xlnt[:, dc * 512:(dc + 1) * 512]
                                            ).then_inc(s_st.h, 16)
                        with g.Else():
                            for dc in range(NDC):
                                g.dma_start(out=ap_shx[dc, :, OWN + t * 512: OWN + (t + 1) * 512],
                                            in_=xlnt[:, dc * 512:(dc + 1) * 512]
                                            ).then_inc(s_st.h, 16)
                        return g.nop()

                    P.op("gpsimd", ship, waits=[(s_act, s_act.n), (s_pool, 3)],
                         bump=[(s_st, 16 * NDC)])
                P.op("gpsimd", lambda g: g.nop(),
                     waits=[(s_st, s_st.n), (s_y, s_y.n)], inc=s_x1)
            return s_x1


        hT_b = hT.ap()
        qt_sb = hT_b[:, 0:4096]
        kt_sb = hT_b[:, 4096:8192]
        vfm_sb = hT_b[:, 8192:12288]
        vt_sb = hT_b[:, 12288:16384]
        z_sb = hT_b[:, 16384:20480]
        xs_sb = [hT_b[:, 20480:28672], hT_b[:, 28672:36864]]
        scb = [satt.ap(), xlnt.ap().bitcast(F32)]
        p4buf = [xch.ap().bitcast(BF16), hT_b[:, 20480:36864]]
        pbb_stage = [pb[4].ap().bitcast(BF16), pb[5].ap().bitcast(BF16)]
        awf2 = [wf32.ap()[:, 0:2048], wf32.ap()[:, 2048:4096]]
        aw_b = [wbf[0].ap()[:, 0:2048], wbf[1].ap()[:, 0:2048], wbf[2].ap()[:, 0:2048]]
        pt4 = [wbf[3].ap()[:, 0:1024], wbf[3].ap()[:, 1024:2048]]
        mt = [sq.ap()[:, 0:512], sq.ap()[:, 512:1024],
              sqd.ap()[:, 0:512], sqd.ap()[:, 512:1024]]
        sat_b = satt.ap().bitcast(BF16)
        pbb4 = pb[4].ap().bitcast(BF16)
        pbb3 = pb[3].ap().bitcast(BF16)

        def barrier(tag):
            allw = [(s, s.n) for s in (s_in, s_w, s_cvt, s_pe, s_act, s_dve,
                                       s_pool, s_st, s_cache, s_ld2, s_y)]
            for eng in ("sync", "gpsimd", "scalar", "vector", "tensor"):
                P.op(eng, lambda e: e.nop(), waits=allw)

        def flag_spin(sync_id):
            r0 = sync_id * 2

            def fn(g):
                with g.If_eq(regs["preg"], 0):
                    g.dma_start(out=ap_shf[r0:r0 + 1, :], in_=sb_magic[:, :]
                                ).then_inc(s_st.h, 16)
                with g.Else():
                    g.dma_start(out=ap_shf[r0 + 1:r0 + 2, :], in_=sb_magic[:, :]
                                ).then_inc(s_st.h, 16)

                def cond():
                    g.reg_alu(regs["rwait"], regs["rwait"], 16, ALU.add)
                    with g.If_eq(regs["preg"], 0):
                        g.dma_start(out=sb_flag[:, :], in_=ap_shf[r0 + 1:r0 + 2, 0:1]
                                    ).then_inc(spoll.h, 16)
                    with g.Else():
                        g.dma_start(out=sb_flag[:, :], in_=ap_shf[r0:r0 + 1, 0:1]
                                    ).then_inc(spoll.h, 16)
                    g.wait_ge(spoll.h, regs["rwait"])
                    g.reg_load(regs["rflag"], sb_flag[0:1, 0:1])
                    g.reg_alu(regs["rcond"], regs["rflag"], regs["rmagic"], ALU.not_equal)
                    return regs["rcond"]

                with g.While(cond):
                    g.nop()
                return g.nop()

            P.op("gpsimd", fn,
                 waits=[(s_st, s_st.n), (s_pool, s_pool.n)], inc=s_x,
                 bump=[(s_st, 16)])

        def emit_attention():
            x_dep = (s_x, s_x.n)
            mtok = []
            for qc4 in range(4):
                t1 = P.op("gpsimd", lambda g, qc4=qc4: g.memset(mt[qc4][:, :], 0.0),
                          waits=[(s_act, s_act.n), (s_dve, s_dve.n), (s_pool, s_pool.n)],
                          inc=s_pool)
                mtok.append(P.op("gpsimd", lambda g, qc4=qc4: g.affine_select(
                    out=mt[qc4][:, :], in_=mt[qc4][:, :], pattern=[[-1, 512]],
                    compare_op=ALU.is_ge, fill=regs["rfill"], base=128 * qc4,
                    channel_multiplier=1), waits=[t1], inc=s_pool))

            aw_free = [None, None, None]
            awf_free = [None, None]
            prev_vt_evac = None
            prev_vtrans_last = None
            prev_qk_last = None
            prev_pv_last = None
            prev_trans_p4 = [None, None]
            unit_exp = {}
            unit_mult = {}
            ug = 0
            bank_mm = {b: None for b in range(8)}
            bank_ev = {b: None for b in range(8)}
            stg_ev = {0: None, 1: None}
            pt_free = [None, None]
            ST = {"zc_prev": None, "last_pv": None, "prev_ship": None}
            cvt0 = (s_cvt, s_cvt.n)

            for h in range(HLOC):
                P.scope = f"attn_h{h}"
                aw_cvt = []
                for i, apw in enumerate((ap_wq, ap_wk, ap_wv)):
                    half = i % 2
                    wl = [w for w in (awf_free[half],) if w is not None]
                    if h == 0:
                        wl.append(cvt0)
                    tl = P.op("scalar", lambda s, apw=apw, h=h, half=half: s.dma_start(
                        out=awf2[half].rearrange("p (dc e) -> p dc e", dc=16),
                        in_=apw[:, :, h * 128:(h + 1) * 128].rearrange("dc p e -> p dc e")),
                        waits=wl, inc=s_ld2, amt=16)
                    wl = [tl]
                    if aw_free[i] is not None:
                        wl.append(aw_free[i])
                    ct = P.op("vector", lambda v, i=i, half=half: v.tensor_copy(
                        out=aw_b[i], in_=awf2[half]), waits=wl, inc=s_dve)
                    aw_cvt.append(ct)
                    awf_free[half] = ct
                slab_mm = {}
                evQ = {}
                evK = {}
                evV = {}
                vt_ev = {}
                vtrans_last = None
                for s5 in range(8):
                    buf = s5 % 2
                    wl = [x_dep]
                    if s5 >= 2:
                        wl.append(slab_mm[s5 - 2])
                    elif prev_trans_p4[1] is not None:
                        wl.append(prev_trans_p4[1])
                    ld = P.op("sync", lambda sy, s5=s5, buf=buf: sy.dma_start(
                        out=xs_sb[buf].rearrange("p (dc s) -> p dc s", dc=16),
                        in_=ap_shx[:, :, s5 * 512:(s5 + 1) * 512].rearrange("dc p s -> p dc s")),
                        waits=wl, inc=s_w, amt=16)
                    tk = None
                    for bi in range(3):
                        for dc in range(NDC):
                            wl = []
                            if dc == 0:
                                wl = [ld, aw_cvt[bi]]
                                if bank_ev[bi] is not None:
                                    wl += bank_ev[bi]
                            tk = P.op("tensor", lambda t_, bi=bi, dc=dc, buf=buf: t_.matmul(
                                pb[bi][:, :], aw_b[bi][:, dc * 128:(dc + 1) * 128],
                                xs_sb[buf][:, dc * 512:(dc + 1) * 512],
                                start=(dc == 0), stop=(dc == NDC - 1)),
                                waits=wl, inc=(s_pe if dc == NDC - 1 else None))
                        bank_mm[bi] = tk
                    slab_mm[s5] = tk
                    for bi, dst, dd in ((0, qt_sb, evQ), (1, kt_sb, evK), (2, vfm_sb, evV)):
                        wl = [bank_mm[bi]]
                        if h > 0 and s5 == 0:
                            wl.append(prev_qk_last if bi < 2 else prev_vtrans_last)
                        if bi == 0:
                            tk = P.op("scalar", lambda s, s5=s5: s.activation(
                                qt_sb[:, s5 * 512:(s5 + 1) * 512], pb[0][:, :], AF.Copy,
                                scale=SCALE), waits=wl, inc=s_act)
                        else:
                            tk = P.op("scalar", lambda s, bi=bi, dst=dst, s5=s5: s.activation(
                                dst[:, s5 * 512:(s5 + 1) * 512], pb[bi][:, :], AF.Copy),
                                waits=wl, inc=s_act)
                        dd[s5] = tk
                        bank_ev[bi] = [tk]
                    grp = s5
                    for k in range(4):
                        wl = []
                        if k == 0:
                            wl = [evV[grp]]
                            if grp > 0:
                                wl.append(vt_ev[grp - 1])
                            elif prev_vt_evac is not None:
                                wl.append(prev_vt_evac)
                        tk = P.op("tensor", lambda t, grp=grp, k=k: t.transpose(
                            pbb3[:, k * 128:(k + 1) * 128],
                            vfm_sb[:, (grp * 4 + k) * 128:(grp * 4 + k + 1) * 128],
                            ident[:, :]), waits=wl, inc=(s_pe if k == 3 else None))
                    vtrans_last = tk
                    wl = [tk]
                    if grp == 0 and prev_pv_last is not None:
                        wl.append(prev_pv_last)
                    vt_ev[grp] = P.op("scalar", lambda s, grp=grp: s.activation(
                        vt_sb[:, grp * 512:(grp + 1) * 512], pbb3[:, 0:512], AF.Copy),
                        waits=wl, inc=s_act)
                aw_free = [bank_mm[0], bank_mm[1], bank_mm[2]]

                trans_last_qt = {}
                zc_tok = {}
                qk_banks = (0, 1, 2, 6)
                qki = 0
                last_qk = None

                class PVctx:
                    def __init__(self, qt):
                        self.qt = qt
                        self.nkc = (qt + 1) * 4
                        self.npair = self.nkc // 2
                        self.qpar = qt % 2
                        self.mults = [unit_mult[ug + qt * 4 + i] for i in range(4)]
                        self.ev = {}
                        self.seq = [("t", 0)]
                        for p in range(1, self.npair):
                            self.seq.append(("t", p))
                            self.seq.append(("m", 2 * (p - 1)))
                            self.seq.append(("m", 2 * (p - 1) + 1))
                        self.seq.append(("m", self.nkc - 2))
                        self.seq.append(("m", self.nkc - 1))
                        self.i = 0

                    def step(self):
                        kind, arg = self.seq[self.i]
                        self.i += 1
                        if kind == "t":
                            self._trans(arg)
                        else:
                            self._mm(arg)
                        return self.i >= len(self.seq)

                    def _trans(self, p):
                        j = p % 2
                        last = None
                        for kk in range(8):
                            kc = 2 * p + kk // 4
                            x4 = kk % 4
                            wl = []
                            if kk == 0:
                                if p == 0:
                                    wl += self.mults
                                if stg_ev[j] is not None:
                                    wl.append(stg_ev[j])
                            last = P.op("tensor", lambda t, kc=kc, x4=x4, kk=kk, j=j,
                                        qp=self.qpar: t.transpose(
                                            pbb_stage[j][:, (kk // 4) * 512 + x4 * 128:
                                                         (kk // 4) * 512 + x4 * 128 + 128],
                                            p4buf[qp][:, x4 * 4096 + kc * 128:
                                                      x4 * 4096 + kc * 128 + 128],
                                            ident[:, :]),
                                        waits=wl, inc=(s_pe if kk == 7 else None))
                        if p == self.npair - 1:
                            trans_last_qt[self.qt] = last
                        wl = [last]
                        if pt_free[j] is not None:
                            wl.append(pt_free[j])
                        e = P.op("vector", lambda v, j=j: v.tensor_copy(
                            out=pt4[j], in_=pbb_stage[j][:, 0:1024]), waits=wl, inc=s_dve)
                        self.ev[p] = e
                        stg_ev[j] = e

                    def _mm(self, kc):
                        p = kc // 2
                        j = p % 2
                        wl = []
                        if kc % 2 == 0:
                            wl.append(self.ev[p])
                        else:
                            self.ev.pop(p, None)
                        if kc == 0 and ST["zc_prev"] is not None:
                            wl.append(ST["zc_prev"])
                        tk = P.op("tensor", lambda t_, kc=kc, j=j, nkc=self.nkc: t_.matmul(
                            pb[7][:, :], vt_sb[:, kc * 128:(kc + 1) * 128],
                            pt4[j][:, (kc % 2) * 512:(kc % 2) * 512 + 512],
                            start=(kc == 0), stop=(kc == nkc - 1)),
                            waits=wl, inc=s_pe)
                        if kc % 2 == 1:
                            pt_free[j] = tk
                        ST["last_pv"] = tk
                        if kc == self.nkc - 1:
                            wl = [tk]
                            if self.qt == 0 and ST["prev_ship"] is not None:
                                wl.append(ST["prev_ship"])
                            zc = P.op("scalar", lambda s, qt=self.qt: s.activation(
                                z_sb[:, qt * 512:(qt + 1) * 512], pb[7][:, :], AF.Copy),
                                waits=wl, inc=s_act)
                            zc_tok[self.qt] = zc
                            ST["zc_prev"] = zc

                pvq = deque()

                def pump(n):
                    for _ in range(n):
                        if pvq:
                            if pvq[0].step():
                                pvq.popleft()

                for qt in range(8):
                    for qc4 in range(4):
                        u = ug + qt * 4 + qc4
                        par = u % 2
                        nk = (qt + 1) * 512
                        qi = qt * 4 + qc4
                        c0 = (u % 4) * 16
                        ev_last_dve = None
                        ev_last_act = None
                        for kt in range(qt + 1):
                            b = qk_banks[qki % 4]
                            qki += 1
                            wl = []
                            if bank_ev[b] is not None:
                                wl += bank_ev[b]
                            if kt == 0:
                                wl += [evQ[qt], evK[qt]]
                            mmtk = P.op("tensor", lambda t_, qi=qi, kt=kt, b=b: t_.matmul(
                                pb[b][:, :], qt_sb[:, qi * 128:(qi + 1) * 128],
                                kt_sb[:, kt * 512:(kt + 1) * 512], start=True, stop=True),
                                waits=wl, inc=s_pe)
                            bank_mm[b] = mmtk
                            last_qk = mmtk
                            wl = [mmtk]
                            if (u - 2) in unit_exp:
                                wl.append(unit_exp[u - 2])
                            if kt == qt:
                                e = P.op("vector", lambda v, kt=kt, b=b, qc4=qc4, par=par:
                                         v.tensor_tensor(
                                             scb[par][:, kt * 512:(kt + 1) * 512],
                                             pb[b][:, :], mt[qc4][:, :], ALU.add),
                                         waits=wl + [mtok[qc4]], inc=s_dve)
                                ev_last_dve = e
                            else:
                                e = P.op("scalar", lambda s, kt=kt, b=b, par=par: s.activation(
                                    scb[par][:, kt * 512:(kt + 1) * 512], pb[b][:, :],
                                    AF.Copy), waits=wl, inc=s_act)
                                ev_last_act = e
                            wl = [e]
                            if kt == 0 and (u - 4) in unit_mult:
                                wl.append(unit_mult[u - 4])
                            last_mk = P.op("vector", lambda v, kt=kt, par=par, co=c0 + kt:
                                           v.tensor_reduce(
                                               lgt[:, co:co + 1],
                                               scb[par][:, kt * 512:(kt + 1) * 512],
                                               AX.X, ALU.max),
                                           waits=wl, inc=s_dve)
                            bank_ev[b] = [e]
                            pump(2)
                        mx = P.op("vector", lambda v, c0=c0, n1=qt + 1: v.tensor_reduce(
                            lgt[:, c0 + 8:c0 + 9], lgt[:, c0:c0 + n1], AX.X, ALU.max,
                            negate=True), waits=[last_mk], inc=s_dve)
                        qpar = qt % 2
                        wl = [mx]
                        if ev_last_act is not None:
                            wl.append(ev_last_act)
                        if qt >= 2:
                            wl.append(trans_last_qt[qt - 2])
                        elif prev_trans_p4[qpar] is not None:
                            wl.append(prev_trans_p4[qpar])
                        if qt == 1:
                            wl.append(slab_mm[7])
                        ex = P.op("scalar", lambda s, qpar=qpar, qc4=qc4, nk=nk, c0=c0, par=par:
                                  s.activation(
                                      p4buf[qpar][:, qc4 * 4096: qc4 * 4096 + nk],
                                      scb[par][:, 0:nk], AF.Exp,
                                      bias=lgt[:, c0 + 8:c0 + 9],
                                      accum_out=lgt[:, c0 + 9:c0 + 10]),
                                  waits=wl, inc=s_act)
                        unit_exp[u] = ex
                        rc = P.op("vector", lambda v, c0=c0: v.reciprocal(
                            lgt[:, c0 + 10:c0 + 11], lgt[:, c0 + 9:c0 + 10]),
                            waits=[ex], inc=s_dve)
                        ml = P.op("vector", lambda v, qpar=qpar, qc4=qc4, nk=nk, c0=c0:
                                  v.tensor_scalar(
                                      p4buf[qpar][:, qc4 * 4096: qc4 * 4096 + nk],
                                      p4buf[qpar][:, qc4 * 4096: qc4 * 4096 + nk],
                                      lgt[:, c0 + 10:c0 + 11], None, ALU.mult),
                                  waits=[rc], inc=s_dve)
                        unit_mult[u] = ml
                        pump(1)
                    pvq.append(PVctx(qt))
                while pvq:
                    if pvq[0].step():
                        pvq.popleft()

                prev_qk_last = last_qk
                prev_pv_last = ST["last_pv"]
                prev_vtrans_last = vtrans_last
                prev_vt_evac = vt_ev[7]
                prev_trans_p4 = [trans_last_qt[6], trans_last_qt[7]]
                ug += 32

                def shipz(g, h=h):
                    with g.If_eq(regs["preg"], 0):
                        g.dma_start(out=ap_shz[h], in_=z_sb).then_inc(s_st.h, 16)
                    with g.Else():
                        g.dma_start(out=ap_shz[HLOC + h], in_=z_sb).then_inc(s_st.h, 16)
                    return g.nop()

                P.op("gpsimd", shipz, waits=[zc_tok[7], (s_pool, s_pool.n)],
                     bump=[(s_st, 16)])
                ST["prev_ship"] = (s_st, s_st.n)

        def emit_outproj():
            x_dep = (s_x, s_x.n)
            wo_bf = hT_b[:, 0:32768]
            zsl = []
            for hh in range(2):
                zsl.append(hT_b[:, 32768 + hh * 2048: 32768 + (hh + 1) * 2048])
            for hh in range(8):
                zsl.append(wbf[hh // 2].ap()[:, (hh % 2) * 2048:(hh % 2 + 1) * 2048])
            for hh in range(4):
                zsl.append(xlnt.ap()[:, hh * 2048:(hh + 1) * 2048])
            sat_b = satt.ap().bitcast(BF16)
            zsl.append(sat_b[:, 0:2048])
            zsl.append(sat_b[:, 2048:4096])
            stg = sat_b[:, 4096:6144]

            for h in range(16):
                P.op("scalar", lambda s, h=h: s.dma_start(
                    out=wf32[:, 0:2048], in_=ap_wo[h]),
                    waits=[(s_cvt, s_cvt.n), x_dep], inc=s_ld2, amt=16)
                P.op("vector", lambda v, h=h: v.tensor_copy(
                    out=wo_bf[:, h * 2048:(h + 1) * 2048], in_=wf32[:, 0:2048]),
                    waits=[(s_ld2, s_ld2.n)], inc=s_cvt)

                def ldz(g, h=h):
                    with g.If_eq(regs["preg"], 0):
                        g.dma_start(out=zsl[h], in_=ap_shz[h, :, 0:OWN]
                                    ).then_inc(s_in.h, 16)
                    with g.Else():
                        g.dma_start(out=zsl[h], in_=ap_shz[h, :, OWN:SEQ]
                                    ).then_inc(s_in.h, 16)
                    return g.nop()

                P.op("gpsimd", ldz, waits=[x_dep], bump=[(s_in, 16)])

            for t in range(NT):
                P.scope = f"outproj_t{t}"
                for sc4 in range(4):
                    sc = t * 4 + sc4
                    xsl = xch.ap()[:, sc4 * D:(sc4 + 1) * D]
                    P.op("gpsimd", lambda g, sc=sc, xsl=xsl: g.dma_start(
                        out=xsl, in_=ap_res1[sc * 128:(sc + 1) * 128, :]),
                        waits=[(s_dve, s_dve.n), (s_pe, s_pe.n), (s_st, s_st.n)],
                        inc=s_in, amt=16)
                    for d5 in range(ND5):
                        bk = (sc4 * ND5 + d5) % 4
                        for h in range(16):
                            P.op("tensor", lambda t_, h=h, sc=sc, d5=d5, bk=bk: t_.matmul(
                                pb[bk][:, :],
                                zsl[h][:, sc * 128:(sc + 1) * 128],
                                wo_bf[:, h * 2048 + d5 * 512: h * 2048 + d5 * 512 + 512],
                                start=(h == 0), stop=(h == 15)),
                                waits=([(s_in, s_in.n), (s_cvt, s_cvt.n),
                                        (s_dve, s_dve.n)] if h == 0 else []),
                                inc=(s_pe if h == 15 else None))
                        P.op("vector", lambda v, sc4=sc4, d5=d5, bk=bk, xsl=xsl: v.tensor_tensor(
                            xsl[:, d5 * 512:(d5 + 1) * 512], pb[bk][:, :],
                            xsl[:, d5 * 512:(d5 + 1) * 512], ALU.add),
                            waits=[(s_pe, s_pe.n), (s_in, s_in.n)], inc=s_dve)
                    ln_stats_chunk(xsl, sc4, (s_dve, s_dve.n))
                ln_batch_rstd()
                for sc4 in range(4):
                    sc = t * 4 + sc4
                    xsl = xch.ap()[:, sc4 * D:(sc4 + 1) * D]
                    P.op("gpsimd", lambda g, sc=sc, xsl=xsl: g.dma_start(
                        out=ap_res2[sc * 128:(sc + 1) * 128, :], in_=xsl),
                        waits=[(s_dve, s_dve.n), (s_act, s_act.n)], inc=s_st, amt=16)
                    P.op("vector", lambda v, sc4=sc4, xsl=xsl: v.tensor_scalar(
                        xtm[:, :], xsl, stats[:, 16 + sc4:17 + sc4],
                        stats[:, 32 + sc4:33 + sc4], ALU.subtract, ALU.mult),
                        waits=[(s_dve, s_dve.n), (s_act, s_act.n), (s_pe, s_pe.n)],
                        inc=s_dve)
                    for grp in range(4):
                        for k in range(4):
                            dc = grp * 4 + k
                            P.op("tensor", lambda t_, dc=dc, k=k: t_.transpose(
                                pbb4[:, k * 128:(k + 1) * 128],
                                xtm.ap()[:, dc * 128:(dc + 1) * 128], ident[:, :]),
                                waits=([(s_dve, s_dve.n), (s_act, s_act.n)]
                                       if k == 0 else []),
                                inc=(s_pe if k == 3 else None))
                        P.op("scalar", lambda s, grp=grp: s.activation(
                            stg[:, grp * 512:(grp + 1) * 512], pbb4[:, 0:512], AF.Copy),
                            waits=[(s_pe, s_pe.n), (s_st, s_st.n)], inc=s_act)
                    for dc in range(NDC):
                        P.op("gpsimd", lambda g, dc=dc, sc=sc: g.dma_start(
                            out=ap_xln2t[dc, :, sc * 128:(sc + 1) * 128],
                            in_=stg[:, dc * 128:(dc + 1) * 128]),
                            waits=[(s_act, s_act.n)], inc=s_st, amt=16)

        emit_ffn(1)
        if build == "full":
            P.scope = "spin1"
            flag_spin(0)
            barrier("p2")
            emit_attention()
            P.scope = "spin2"
            flag_spin(1)
            barrier("p3")
            emit_outproj()
            barrier("p4")
            emit_ffn(2)

        if debug_out == "cache":
            P.op("scalar", lambda s: s.dma_start(
                out=ap_y[1024:1152, 0:2048],
                in_=ap_cdown[0, 0].bitcast(F32)),
                waits=[(s_cache, s_cache.n), (s_st, s_st.n)], inc=s_y, amt=16)
            P.op("scalar", lambda s: s.dma_start(
                out=ap_y[1152:1280, 0:2048],
                in_=ap_cgate[0].bitcast(F32)),
                waits=[(s_cache, s_cache.n)], inc=s_y, amt=16)

        def final_wait(g):
            return g.nop()

        P.op("gpsimd", final_wait,
             waits=[(s_y, s_y.n), (s_st, s_st.n), (s_cache, s_cache.n)])
        P.emit()
    return nc



def _layout_inputs(L0_attn_x, L0_attn_W_Q, L0_attn_W_K, L0_attn_W_V, L0_attn_W_O,
                   L0_ffn_router_weight, L0_ffn_sh_gate, L0_ffn_sh_up, L0_ffn_sh_down,
                   L0_ffn_r_gate, L0_ffn_r_up, L0_ffn_r_down):
    c = np.ascontiguousarray
    f = np.asarray
    gate_t = c(f(L0_ffn_sh_gate).T.reshape(16, 128, 32, 256).transpose(2, 0, 1, 3))
    up_t = c(f(L0_ffn_sh_up).T.reshape(16, 128, 32, 256).transpose(2, 0, 1, 3))
    down_t = c(f(L0_ffn_sh_down).T.reshape(64, 128, 4, 512).transpose(2, 0, 1, 3))
    rgate_t = c(f(L0_ffn_r_gate).T.reshape(16, 128, 4, 256).transpose(2, 0, 1, 3))
    rup_t = c(f(L0_ffn_r_up).T.reshape(16, 128, 4, 256).transpose(2, 0, 1, 3))
    rdown_t = c(f(L0_ffn_r_down).T.reshape(8, 128, 4, 512).transpose(2, 0, 1, 3))
    wr_t = c(f(L0_ffn_router_weight).T.reshape(16, 128, 8))
    wo_t = c(f(L0_attn_W_O).transpose(0, 2, 1))
    wq, wk, wv = f(L0_attn_W_Q), f(L0_attn_W_K), f(L0_attn_W_V)
    magic = np.full((1, 64), np.random.randint(1, 2**31 - 1), dtype=np.uint32)
    x = f(L0_attn_x)
    in_maps = []
    for core in range(8):
        b, j = core // 2, core % 2
        hs = slice(8 * j, 8 * j + 8)
        in_maps.append({
            "x": c(x[b, j * 2048:(j + 1) * 2048, :]),
            "gate_t": gate_t, "up_t": up_t, "down_t": down_t,
            "rgate_t": rgate_t, "rup_t": rup_t, "rdown_t": rdown_t,
            "wr_t": wr_t,
            "wq_t": c(wq[hs].transpose(2, 0, 1).reshape(16, 128, 8 * 128)),
            "wk_t": c(wk[hs].transpose(2, 0, 1).reshape(16, 128, 8 * 128)),
            "wv_t": c(wv[hs].transpose(2, 0, 1).reshape(16, 128, 8 * 128)),
            "wo_t": wo_t,
            "magic": magic,
        })
    return in_maps


def kernel(**inputs):
    in_maps = _layout_inputs(**inputs)
    nc = build_program()
    res = run_bass_kernel_spmd(nc, in_maps, core_ids=list(range(8)))
    out = np.empty((4, 4096, 2048), dtype=np.float32)
    for core in range(8):
        b, j = core // 2, core % 2
        out[b, j * 2048:(j + 1) * 2048, :] = res.results[core]["y"]
    return out


# version 11
# speedup vs baseline: 1.5544x; 1.5544x over previous
import numpy as np
import concourse.bass as bass
import concourse.mybir as mybir
from concourse.bass_utils import run_bass_kernel_spmd
from contextlib import ExitStack
from collections import deque

F32 = mybir.dt.float32
BF16 = mybir.dt.bfloat16
U32 = mybir.dt.uint32
ALU = mybir.AluOpType
AF = mybir.ActivationFunctionType
AX = mybir.AxisListType

D = 2048
SEQ = 4096
OWN = 2048
NT = 4
NDC = 16
INTER = 8192
MOE = 1024
NI = 72
NI_SH = 64
ND5 = 4
HLOC = 8
E = 128
SCALE = float(1.0 / np.sqrt(128.0))
RSQRT_MAGIC = 0x5F3759DF

BUILD = "full"
DEBUG_OUT = "none"


class Sem:
    __slots__ = ("h", "n")

    def __init__(self, h):
        self.h = h
        self.n = 0


class Prog:
    def __init__(self, nc, stack):
        self.nc = nc
        self.ops = {k: [] for k in ("sync", "gpsimd", "scalar", "vector", "tensor")}
        self.stack = stack
        self.sems = {}
        self.scope = "init"

    def sem(self, name):
        s = self.sems.get(name)
        if s is None:
            s = Sem(self.stack.enter_context(self.nc.semaphore(name)))
            self.sems[name] = s
        return s

    def op(self, eng, fn, waits=(), inc=None, amt=1, incs=(), bump=()):
        wl = []
        for w in waits:
            s, v = (w if isinstance(w, tuple) else (w, w.n))
            if v > 0:
                wl.append((s.h, v))
        il = list(incs)
        if inc is not None:
            il.append((inc, amt))
        self.ops[eng].append((fn, wl, [(s.h, a) for s, a in il], self.scope))
        for s, a in il:
            s.n += a
        for s, a in bump:
            s.n += a
        if inc is not None:
            return (inc, inc.n)
        return None

    def dve_chain(self, fns, waits=()):
        sd = self.sem("s_dve")
        for i, fn in enumerate(fns):
            self.op("vector", fn,
                    waits=(list(waits) + [(sd, sd.n)]) if i == 0 else [(sd, sd.n)],
                    inc=sd)

    def emit(self):
        block = self.stack.enter_context(self.nc.Block())
        for engname in ("sync", "gpsimd", "scalar", "vector", "tensor"):
            def mk(engname):
                wm = {}

                def body(eng):
                    cur = None
                    for fn, wl, il, sc in self.ops[engname]:
                        if sc != cur:
                            if cur is not None:
                                self.nc.pop_named_scope(cur)
                            self.nc.push_named_scope(sc)
                            cur = sc
                        for h, v in wl:
                            if wm.get(h, 0) < v:
                                eng.wait_ge(h, v)
                                wm[h] = v
                        ins = fn(eng)
                        for h, a in il:
                            ins = ins.then_inc(h, a)
                    if cur is not None:
                        self.nc.pop_named_scope(cur)
                return body
            getattr(block, engname)(mk(engname))


def build_program(build=BUILD, debug_out=DEBUG_OUT):
    nc = bass.Bass("TRN2", target_bir_lowering=False)

    ap_x = nc.dram_tensor("x", [OWN, D], F32, kind="ExternalInput").ap()
    ap_gate = nc.dram_tensor("gate_t", [32, 16, 128, 256], F32, kind="ExternalInput").ap()
    ap_up = nc.dram_tensor("up_t", [32, 16, 128, 256], F32, kind="ExternalInput").ap()
    ap_down = nc.dram_tensor("down_t", [4, 64, 128, 512], F32, kind="ExternalInput").ap()
    ap_rgate = nc.dram_tensor("rgate_t", [4, 16, 128, 256], F32, kind="ExternalInput").ap()
    ap_rup = nc.dram_tensor("rup_t", [4, 16, 128, 256], F32, kind="ExternalInput").ap()
    ap_rdown = nc.dram_tensor("rdown_t", [4, 8, 128, 512], F32, kind="ExternalInput").ap()
    ap_wr = nc.dram_tensor("wr_t", [16, 128, 8], F32, kind="ExternalInput").ap()
    ap_wq = nc.dram_tensor("wq_t", [16, 128, HLOC * 128], F32, kind="ExternalInput").ap()
    ap_wk = nc.dram_tensor("wk_t", [16, 128, HLOC * 128], F32, kind="ExternalInput").ap()
    ap_wv = nc.dram_tensor("wv_t", [16, 128, HLOC * 128], F32, kind="ExternalInput").ap()
    ap_wo = nc.dram_tensor("wo_t", [16, 128, D], F32, kind="ExternalInput").ap()
    ap_magic = nc.dram_tensor("magic", [1, 64], U32, kind="ExternalInput").ap()
    ap_y = nc.dram_tensor("y", [OWN, D], F32, kind="ExternalOutput").ap()
    ap_pid = nc.partition_id_tensor.ap()

    ap_res1 = nc.dram_tensor("res1", [OWN, D], F32, kind="Internal").ap()
    ap_res2 = nc.dram_tensor("res2", [OWN, D], F32, kind="Internal").ap()
    ap_xln2t = nc.dram_tensor("xln2t", [16, 128, OWN], BF16, kind="Internal").ap()
    ap_cgate = nc.dram_tensor("cgate", [32, 128, 4096], BF16, kind="Internal").ap()
    ap_cup = nc.dram_tensor("cup", [32, 128, 4096], BF16, kind="Internal").ap()
    ap_cdown = nc.dram_tensor("cdown", [4, 9, 128, 4096], BF16, kind="Internal").ap()
    ap_crgate = nc.dram_tensor("crgate", [4, 128, 4096], BF16, kind="Internal").ap()
    ap_crup = nc.dram_tensor("crup", [4, 128, 4096], BF16, kind="Internal").ap()
    ap_crdown = nc.dram_tensor("crdown", [4, 1, 128, 4096], BF16, kind="Internal").ap()
    ap_shx = nc.dram_tensor("shx", [16, 128, SEQ], BF16, kind="Internal", addr_space="Shared").ap()
    ap_shz = nc.dram_tensor("shz", [16, 128, SEQ], BF16, kind="Internal", addr_space="Shared").ap()
    ap_shf = nc.dram_tensor("shf", [8, 64], U32, kind="Internal", addr_space="Shared").ap()

    stack = ExitStack()
    with stack:
        e = stack.enter_context
        P = Prog(nc, stack)

        ident = e(nc.sbuf_tensor("ident", [128, 128], BF16))
        wr_bf = e(nc.sbuf_tensor("wr_bf", [128, 128], BF16))
        sb_pid = e(nc.sbuf_tensor("sb_pid", [1, 1], U32))
        sb_magic = e(nc.sbuf_tensor("sb_magic", [1, 64], U32))
        sb_flag = e(nc.sbuf_tensor("sb_flag", [1, 1], U32))

        xch = e(nc.sbuf_tensor("xch", [128, 4 * D], F32))
        sq = e(nc.sbuf_tensor("sq", [128, 1024], F32))
        sqd = e(nc.sbuf_tensor("sqd", [128, 1024], F32))
        stats = e(nc.sbuf_tensor("stats", [128, 48], F32))
        lgt = e(nc.sbuf_tensor("lgt", [128, 64], F32))
        xlnt = e(nc.sbuf_tensor("xlnt", [128, NDC * 512], BF16))
        xtm = e(nc.sbuf_tensor("xtm", [128, D], BF16))
        hT = e(nc.sbuf_tensor("hT", [128, NI * 512], BF16))
        wf32 = e(nc.sbuf_tensor("wf32", [128, 4096], F32))
        wbf = [e(nc.sbuf_tensor(f"wbf{i}", [128, 4096], BF16)) for i in range(4)]
        satt = e(nc.sbuf_tensor("satt", [128, 4096], F32))
        sg_sb = [e(nc.sbuf_tensor(f"sg{i}", [128, 512], F32)) for i in range(2)]
        tmp5 = [e(nc.sbuf_tensor(f"tmp5_{i}", [128, 512], F32)) for i in range(2)]

        pb = [e(nc.psum_tensor(f"pb{i}", [128, 512], F32)) for i in range(8)]
        pbb4 = pb[4].ap().bitcast(BF16)

        s_in = P.sem("s_in")
        s_w = P.sem("s_w")
        s_cvt = P.sem("s_cvt")
        s_pe = P.sem("s_pe")
        s_act = P.sem("s_act")
        s_dve = P.sem("s_dve")
        s_pool = P.sem("s_pool")
        s_st = P.sem("s_st")
        s_cache = P.sem("s_cache")
        s_ld2 = P.sem("s_ld2")
        s_y = P.sem("s_y")
        s_x = P.sem("s_x")
        spoll = P.sem("s_poll")

        P.op("gpsimd", lambda g: g.dma_start(out=sb_pid[:, :], in_=ap_pid[0:1, 0:1]), inc=s_st, amt=16)
        P.op("gpsimd", lambda g: g.dma_start(out=sb_magic[:, :], in_=ap_magic[:, :]), inc=s_st, amt=16)
        P.op("gpsimd", lambda g: g.memset(ident[:, :], 1.0), inc=s_pool)
        P.op("gpsimd", lambda g: g.affine_select(out=ident[:, :], in_=ident[:, :],
                                                 pattern=[[1, 128]], compare_op=ALU.is_equal,
                                                 fill=0.0, base=0, channel_multiplier=-1),
             waits=[s_pool], inc=s_pool)
        P.op("sync", lambda sy: sy.dma_start(
            out=wf32[:, 0:128].rearrange("p (a b) -> p a b", a=16),
            in_=ap_wr.rearrange("a p b -> p a b")), inc=s_w, amt=16)
        P.op("vector", lambda v: v.tensor_copy(out=wr_bf[:, :], in_=wf32[:, 0:128]),
             waits=[s_w], inc=s_cvt)

        regs = {}

        def setup_regs(g):
            for r in ("preg", "rmagic", "rflag", "rcond", "rwait", "rfill"):
                regs[r] = g.alloc_register(r)
            g.reg_mov(regs["rfill"], -1e9)
            g.reg_load(regs["preg"], sb_pid[0:1, 0:1])
            g.reg_alu(regs["preg"], regs["preg"], 1, ALU.bitwise_and)
            g.reg_load(regs["rmagic"], sb_magic[0:1, 0:1])
            return g.reg_mov(regs["rwait"], 0)

        P.op("gpsimd", setup_regs, waits=[(s_st, 32)], inc=s_pool)

        def ln_stats_chunk(xsl, col, dep, st=None, extra=()):
            if st is None:
                st = stats
            P.op("vector", lambda v: v.tensor_scalar(
                sqd[:, :], xsl[:, 0:1024], 1.0, 0.0, ALU.mult, ALU.add,
                accum_out=st[:, col:col + 1]),
                waits=[dep, (s_dve, s_dve.n)] + list(extra), inc=s_dve)
            dt = P.op("vector", lambda v: v.tensor_scalar(
                sqd[:, :], xsl[:, 1024:2048], 1.0, 0.0, ALU.mult, ALU.add,
                accum_out=st[:, 4 + col:5 + col]), waits=[(s_dve, s_dve.n)], inc=s_dve)
            P.op("scalar", lambda s: s.activation(
                sq[:, :], xsl[:, 0:1024], AF.Square,
                accum_out=st[:, 8 + col:9 + col]),
                waits=[dep, (s_act, s_act.n)] + list(extra), inc=s_act)
            at = P.op("scalar", lambda s: s.activation(
                sq[:, :], xsl[:, 1024:2048], AF.Square,
                accum_out=st[:, 12 + col:13 + col]), waits=[(s_act, s_act.n)], inc=s_act)
            return dt, at

        def ln_batch_rstd(st=None):
            if st is None:
                st = stats
            P.dve_chain([
                lambda v: v.tensor_tensor(st[:, 24:28], st[:, 0:4], st[:, 4:8], ALU.add),
                lambda v: v.tensor_tensor(st[:, 28:32], st[:, 8:12], st[:, 12:16], ALU.add),
                lambda v: v.tensor_scalar(st[:, 16:20], st[:, 24:28], 1.0 / D, None, ALU.mult),
                lambda v: v.tensor_scalar(st[:, 28:32], st[:, 28:32], 1.0 / D, None, ALU.mult),
                lambda v: v.tensor_tensor(st[:, 24:28], st[:, 16:20], st[:, 16:20], ALU.mult),
                lambda v: v.tensor_tensor(st[:, 28:32], st[:, 28:32], st[:, 24:28], ALU.subtract),
                lambda v: v.tensor_scalar(st[:, 28:32], st[:, 28:32], 1.0, 1e-5, ALU.mult, ALU.add),
            ], waits=[(s_act, s_act.n)])
            P.op("scalar", lambda s: s.activation(st[:, 36:40], st[:, 28:32], AF.Ln),
                 waits=[(s_dve, s_dve.n)], inc=s_act)
            return P.op("scalar", lambda s: s.activation(st[:, 32:36], st[:, 36:40],
                                                         AF.Exp, scale=-0.5),
                        waits=[(s_act, s_act.n)], inc=s_act)

        def transpose_chunk(srcap, dst_sb, colbase, dep_dve):
            dstv = dst_sb.ap().rearrange("p (dc f) -> p dc f", dc=NDC)
            srcv = pbb4.rearrange("p (g f) -> p g f", g=8)[:, 0:4, :]
            for grp in range(4):
                for k in range(4):
                    dc = grp * 4 + k
                    P.op("tensor", lambda t, dc=dc, k=k: t.transpose(
                        pbb4[:, k * 128:(k + 1) * 128],
                        srcap[:, dc * 128:(dc + 1) * 128], ident[:, :]),
                        waits=([(s_pool, 2), dep_dve, (s_act, s_act.n)]
                               if k == 0 else []),
                        inc=(s_pe if k == 3 else None))
                P.op("scalar", lambda s, grp=grp: s.activation(
                    dstv[:, grp * 4:(grp + 1) * 4, colbase:colbase + 128],
                    srcv, AF.Copy),
                    waits=[(s_pe, s_pe.n), (s_st, s_st.n)], inc=s_act)

        def emit_ffn(phase):
            first = (phase == 1)
            src_res = ap_x if first else ap_res2
            s_x1 = P.sem(f"sx1_{phase}")
            s_hw = P.sem(f"shw_{phase}")
            s_hr = P.sem(f"shr_{phase}")
            sd_slot = [P.sem(f"sdslot{i}_{phase}") for i in range(2)]
            sd_rdy = [P.sem(f"sdrdy{i}_{phase}") for i in range(2)]
            sd_bank = [P.sem(f"sdbank{i}_{phase}") for i in range(8)]
            sw_slot = [P.sem(f"swslot{i}_{phase}") for i in range(4)]
            sg_bank = [P.sem(f"sgbank{i}_{phase}") for i in range(2)]
            su_bank = [P.sem(f"subank{i}_{phase}") for i in range(2)]
            sw_rdy = [P.sem(f"swrdy{i}_{phase}") for i in range(4)]

            fstage = [wf32.ap()[:, :], satt.ap()[:, :]]
            fqueue = ["sync", "scalar"]
            fqsem = [s_w, s_ld2]
            fst = {"i": 0, "free": [None, None]}

            for t in range(NT):
                P.scope = f"ffn{phase}_t{t}"
                cachepass = first and t == 0
                CACHE_GU, CACHE_DN = True, True
                for sc in range(4):
                    row0 = t * 512 + sc * 128
                    xsl = xch.ap()[:, sc * D:(sc + 1) * D]
                    P.op("gpsimd", lambda g, r=row0, d=xsl: g.dma_start(
                        out=d, in_=src_res[r:r + 128, :]),
                        waits=[(s_x1, s_x1.n)], inc=s_in, amt=16)
                    dep = (s_in, s_in.n)
                    if first:
                        ln_stats_chunk(xsl, sc, dep)
                if first:
                    ln_batch_rstd()
                    for sc in range(4):
                        xsl = xch.ap()[:, sc * D:(sc + 1) * D]
                        P.op("vector", lambda v, sc=sc, xsl=xsl: v.tensor_scalar(
                            xtm[:, :], xsl, stats[:, 16 + sc:17 + sc],
                            stats[:, 32 + sc:33 + sc], ALU.subtract, ALU.mult),
                            waits=[(s_dve, s_dve.n), (s_act, s_act.n), (s_pe, s_pe.n)],
                            inc=s_dve)
                        transpose_chunk(xtm.ap(), xlnt, sc * 128, (s_dve, s_dve.n))
                else:
                    for dc in range(NDC):
                        P.op("scalar", lambda s, dc=dc, t=t: s.dma_start(
                            out=xlnt[:, dc * 512:(dc + 1) * 512],
                            in_=ap_xln2t[dc, :, t * 512:(t + 1) * 512]),
                            waits=[(s_st, s_st.n), (s_pe, s_pe.n)],
                            inc=s_ld2, amt=16)
                xlnt_dep = (s_act, s_act.n) if first else (s_ld2, s_ld2.n)
                if debug_out == "ln" and first and t == 0:
                    P.op("gpsimd", lambda g: g.dma_start(
                        out=ap_y[0:128, 0:48], in_=stats[:, :]),
                        waits=[(s_dve, s_dve.n), (s_act, s_act.n)], inc=s_y, amt=16)
                    P.op("scalar", lambda s: s.activation(
                        sq[:, :], xlnt[:, 0:1024], AF.Copy),
                        waits=[(s_act, s_act.n), (s_dve, s_dve.n)], inc=s_act)
                    P.op("gpsimd", lambda g: g.dma_start(
                        out=ap_y[128:256, 0:1024], in_=sq[:, :]),
                        waits=[(s_act, s_act.n)], inc=s_y, amt=16)
                    P.op("gpsimd", lambda g: g.dma_start(
                        out=ap_y[256:384, :], in_=xch[:, 0:2048]),
                        waits=[(s_in, s_in.n)], inc=s_y, amt=16)
                for sc in range(4):
                    for dc in range(NDC):
                        P.op("tensor", lambda t_, sc=sc, dc=dc: t_.matmul(
                            pb[5][:, 0:8],
                            xlnt[:, dc * 512 + sc * 128: dc * 512 + sc * 128 + 128],
                            wr_bf[:, dc * 8:(dc + 1) * 8],
                            start=(dc == 0), stop=(dc == NDC - 1)),
                            waits=([xlnt_dep, (s_cvt, s_cvt.n), (s_dve, s_dve.n)]
                                   if dc == 0 else []),
                            inc=(s_pe if dc == NDC - 1 else None))
                    P.op("vector", lambda v, sc=sc: v.tensor_copy(
                        out=lgt[:, sc * 8:(sc + 1) * 8], in_=pb[5][:, 0:8]),
                        waits=[(s_pe, s_pe.n)], inc=s_dve)

                fns = []
                for sc in range(4):
                    fns.append(lambda v, sc=sc: v.tensor_reduce(
                        lgt[:, 32 + sc:33 + sc], lgt[:, sc * 8:(sc + 1) * 8], AX.X, ALU.max))
                for sc in range(4):
                    fns.append(lambda v, sc=sc: v.tensor_scalar(
                        lgt[:, 48:56], lgt[:, sc * 8:(sc + 1) * 8],
                        lgt[:, 32 + sc:33 + sc], None, ALU.is_ge))
                    fns.append(lambda v, sc=sc: v.scalar_tensor_tensor(
                        lgt[:, 48:56], lgt[:, 48:56], -2000.0,
                        lgt[:, sc * 8:(sc + 1) * 8], ALU.mult, ALU.add))
                    fns.append(lambda v, sc=sc: v.tensor_reduce(
                        lgt[:, 36 + sc:37 + sc], lgt[:, 48:56], AX.X, ALU.max))
                fns.append(lambda v: v.tensor_tensor(
                    lgt[:, 32:36], lgt[:, 32:36], lgt[:, 36:40], ALU.subtract))
                fns.append(lambda v: v.tensor_scalar(
                    lgt[:, 36:40], lgt[:, 32:36], -1.0, None, ALU.mult))
                P.dve_chain(fns)
                P.op("scalar", lambda s: s.activation(lgt[:, 48:56], lgt[:, 32:40],
                                                      AF.Tanh, scale=0.5),
                     waits=[(s_dve, s_dve.n)], inc=s_act)
                P.dve_chain([
                    lambda v: v.tensor_tensor(lgt[:, 56:60], lgt[:, 48:52],
                                              lgt[:, 52:56], ALU.add),
                    lambda v: v.tensor_scalar(lgt[:, 60:64], lgt[:, 56:60], 0.25, 0.5,
                                              ALU.mult, ALU.add),
                ], waits=[(s_act, s_act.n)])
                comb_dep = (s_dve, s_dve.n)

                if debug_out == "t1" and first and t == 1:
                    P.op("gpsimd", lambda g: g.dma_start(
                        out=ap_y[1024:1152, 0:512],
                        in_=xlnt.ap().bitcast(F32)[:, 0:512]),
                        waits=[(s_act, s_act.n), (s_dve, s_dve.n)], inc=s_y, amt=16)
                blocks = ([(ap_gate, ap_cgate, ap_up, ap_cup, i) for i in range(32)] +
                          [(ap_rgate, ap_crgate, ap_rup, ap_crup, i) for i in range(4)])

                def load_block(bi):
                    agf, agc, auf, auc, i256 = blocks[bi]
                    gslot, uslot = bi % 2, 2 + bi % 2
                    for (af32, acache, slot) in ((agf, agc, gslot), (auf, auc, uslot)):
                        dstb = wbf[slot]
                        if cachepass:
                            j = fst["i"] % 2
                            fst["i"] += 1
                            wl = ([fst["free"][j]] if fst["free"][j] is not None
                                  else ([(s_cvt, s_cvt.n)] if j == 0 else []))
                            ld = P.op(fqueue[j], lambda e, a=af32, i=i256, j=j: e.dma_start(
                                out=fstage[j].rearrange("p (dc ic) -> p dc ic", dc=16),
                                in_=a[i].rearrange("dc p ic -> p dc ic")),
                                waits=wl, inc=fqsem[j], amt=16)
                            cvt = P.op("vector", lambda v, d=dstb, j=j: v.tensor_copy(
                                out=d[:, :], in_=fstage[j]),
                                waits=[ld, (s_pe, s_pe.n), (s_cache, s_cache.n)],
                                inc=sw_rdy[slot])
                            fst["free"][j] = cvt
                            P.op("gpsimd", lambda g, a=acache, i=i256, d=dstb: g.dma_start(
                                out=a[i], in_=d[:, :]),
                                waits=[(sw_rdy[slot], sw_rdy[slot].n)], inc=s_cache, amt=16)
                        else:
                            P.op("sync", lambda sy, a=acache, i=i256, d=dstb: sy.dma_start(
                                out=d[:, :], in_=a[i]),
                                waits=[(s_pe, s_pe.n), (s_cache, s_cache.n)],
                                inc=sw_rdy[slot], amt=16)

                def mm_block(bi):
                    agf, agc, auf, auc, i256 = blocks[bi]
                    gslot, uslot = bi % 2, 2 + bi % 2
                    wvg = (sw_rdy[gslot], sw_rdy[gslot].n)
                    wvu = (sw_rdy[uslot], sw_rdy[uslot].n)
                    for ii in range(2):
                        ic = bi * 2 + ii
                        gb, ub = pb[ic % 2], pb[2 + ic % 2]
                        for dc in range(NDC):
                            P.op("tensor", lambda t_, ii=ii, dc=dc, gb=gb, s=gslot: t_.matmul(
                                gb[:, :],
                                wbf[s][:, dc * 256 + ii * 128: dc * 256 + ii * 128 + 128],
                                xlnt[:, dc * 512:(dc + 1) * 512],
                                start=(dc == 0), stop=(dc == NDC - 1)),
                                waits=([wvg, (sg_bank[ic % 2], sg_bank[ic % 2].n), xlnt_dep]
                                       if dc == 0 else []),
                                inc=None)
                        for dc in range(NDC):
                            last = dc == NDC - 1
                            P.op("tensor", lambda t_, ii=ii, dc=dc, ub=ub, s=uslot: t_.matmul(
                                ub[:, :],
                                wbf[s][:, dc * 256 + ii * 128: dc * 256 + ii * 128 + 128],
                                xlnt[:, dc * 512:(dc + 1) * 512],
                                start=(dc == 0), stop=(dc == NDC - 1)),
                                waits=([wvu, (su_bank[ic % 2], su_bank[ic % 2].n)]
                                       if dc == 0 else []),
                                inc=(s_pe if last else None))
                        sgb = sg_sb[ic % 2]
                        P.op("scalar", lambda s, gb=gb, sgb=sgb: s.activation(
                            sgb[:, :], gb[:, :], AF.Silu),
                            waits=[(s_pe, s_pe.n), (s_hw, s_hw.n)],
                            inc=sg_bank[ic % 2])
                        P.op("vector", lambda v, sgb=sgb, ub=ub, ic=ic: v.scalar_tensor_tensor(
                            hT[:, ic * 512:(ic + 1) * 512], sgb[:, :], 1.0, ub[:, :],
                            ALU.mult, ALU.mult),
                            waits=[(sg_bank[ic % 2], sg_bank[ic % 2].n), (s_hr, s_hr.n)],
                            inc=s_hw)
                        P.op("vector", lambda v: v.engine_nop(), inc=su_bank[ic % 2])

                load_block(0)
                load_block(1)
                for bi in range(len(blocks)):
                    mm_block(bi)
                    if bi + 2 < len(blocks):
                        load_block(bi + 2)
                dep_hw = (s_hw, s_hw.n)

                def dn_load(d5, si, slot):
                    routed = si == 8
                    adf, adc = (ap_rdown, ap_crdown) if routed else (ap_down, ap_cdown)
                    icb = 0 if routed else si
                    dstb = wbf[slot]
                    if cachepass:
                        j = fst["i"] % 2
                        fst["i"] += 1
                        wl = ([fst["free"][j]] if fst["free"][j] is not None
                              else ([(s_cvt, s_cvt.n)] if j == 0 else []))
                        ld = P.op(fqueue[j], lambda e, a=adf, d5=d5, icb=icb, routed=routed,
                                  j=j: e.dma_start(
                            out=fstage[j].rearrange("p (ic d) -> p ic d", ic=8),
                            in_=(a[d5, 0:8] if routed else a[d5, icb * 8:(icb + 1) * 8])
                            .rearrange("ic p dcol -> p ic dcol")),
                            waits=wl, inc=fqsem[j], amt=16)
                        cvt = P.op("vector", lambda v, d=dstb, j=j: v.tensor_copy(
                            out=d[:, :], in_=fstage[j]),
                            waits=[ld, (s_pe, s_pe.n), (s_cache, s_cache.n)],
                            inc=sd_rdy[slot % 2])
                        fst["free"][j] = cvt
                        P.op("gpsimd", lambda g, a=adc, d5=d5, icb=icb, routed=routed, d=dstb:
                             g.dma_start(out=a[d5, 0 if routed else icb], in_=d[:, :]),
                             waits=[(sd_rdy[slot % 2], sd_rdy[slot % 2].n)],
                             inc=s_cache, amt=16)
                    else:
                        P.op("sync", lambda sy, a=adc, d5=d5, icb=icb, routed=routed, d=dstb:
                             sy.dma_start(out=d[:, :], in_=a[d5, 0 if routed else icb]),
                             waits=[(s_pe, s_pe.n), (s_cache, s_cache.n)],
                             inc=sd_rdy[slot % 2], amt=16)

                def dn_mm(d5, si, slot):
                    routed = si == 8
                    dstb = wbf[slot]
                    wv = (sd_rdy[slot % 2], sd_rdy[slot % 2].n)
                    for ii in range(8):
                        ic = (NI_SH + ii) if routed else si * 8 + ii
                        for sc in range(4):
                            bidx = (4 + sc) if routed else sc
                            laststop = (ic == NI_SH - 1 or ic == NI - 1)
                            P.op("tensor", lambda t_, d=dstb, ii=ii, sc=sc, ic=ic, b=bidx: t_.matmul(
                                pb[b][:, :],
                                hT[:, ic * 512 + sc * 128: ic * 512 + sc * 128 + 128],
                                d[:, ii * 512:(ii + 1) * 512],
                                start=(ic == 0 or ic == NI_SH),
                                stop=(ic == NI_SH - 1 or ic == NI - 1)),
                                waits=([wv, dep_hw,
                                        (sd_bank[bidx], sd_bank[bidx].n)]
                                       if ii == 0 else []),
                                inc=(s_pe if sc == 3 and (ii == 7 or laststop) else None))

                for d5 in range(ND5):
                    if d5 == 0:
                        dn_load(0, 0, 0)
                        dn_load(0, 1, 1)
                    for si in range(9):
                        q = d5 * 9 + si
                        dn_mm(d5, si, q % 2)
                        nq = q + 2
                        if nq < ND5 * 9:
                            dn_load(nq // 9, nq % 9, nq % 2)
                    for sc in range(4):
                        xsl = xch.ap()[:, sc * D + d5 * 512: sc * D + (d5 + 1) * 512]
                        t5 = tmp5[sc % 2]
                        P.op("vector", lambda v, sc=sc, t5=t5, xsl=xsl: v.scalar_tensor_tensor(
                            t5[:, :], pb[4 + sc][:, :], lgt[:, 60 + sc:61 + sc], xsl,
                            ALU.mult, ALU.add),
                            waits=[(s_pe, s_pe.n), comb_dep], inc=sd_bank[4 + sc])
                        P.op("vector", lambda v, sc=sc, t5=t5, xsl=xsl: v.scalar_tensor_tensor(
                            xsl, pb[sc][:, :], 0.5, t5[:, :], ALU.mult, ALU.add),
                            waits=[(sd_bank[4 + sc], sd_bank[4 + sc].n)],
                            inc=sd_bank[sc])
                if True:
                    pass
                if debug_out == "t0" and first and t == 0:
                    P.op("scalar", lambda s: s.activation(
                        sq[:, 0:512], pb[0][:, :], AF.Copy),
                        waits=[(s_pe, s_pe.n), (sd_bank[0], sd_bank[0].n)], inc=s_act)
                    P.op("scalar", lambda s: s.activation(
                        sq[:, 512:1024], pb[4][:, :], AF.Copy),
                        waits=[(s_act, s_act.n)], inc=s_act)
                    P.op("gpsimd", lambda g: g.dma_start(
                        out=ap_y[1408:1536, 0:1024], in_=sq[:, 0:1024]),
                        waits=[(s_act, s_act.n)], inc=s_y, amt=16)
                P.op("vector", lambda v: v.engine_nop(),
                     waits=[(sd_bank[i], sd_bank[i].n) for i in range(8)],
                     inc=s_dve)
                P.op("tensor", lambda t_: t_.nop(), incs=[(s_hr, NI)])

                for sc in range(4):
                    row0 = t * 512 + sc * 128
                    xsl = xch.ap()[:, sc * D:(sc + 1) * D]
                    P.op("gpsimd", lambda g, r=row0, xsl=xsl: g.dma_start(
                        out=(ap_res1 if first else ap_y)[r:r + 128, :], in_=xsl),
                        waits=[(s_dve, s_dve.n)],
                        inc=(s_st if first else s_y), amt=16)
                    if debug_out == "x1" and first:
                        P.op("gpsimd", lambda g, r=row0, xsl=xsl: g.dma_start(
                            out=ap_y[r:r + 128, :], in_=xsl),
                            waits=[(s_dve, s_dve.n)], inc=s_y, amt=16)
                    if first and build == "full":
                        ln_stats_chunk(xsl, sc, (s_dve, s_dve.n))
                if first and build == "full":
                    ln_batch_rstd()
                    for sc in range(4):
                        xsl = xch.ap()[:, sc * D:(sc + 1) * D]
                        P.op("vector", lambda v, sc=sc, xsl=xsl: v.tensor_scalar(
                            xtm[:, :], xsl, stats[:, 16 + sc:17 + sc],
                            stats[:, 32 + sc:33 + sc], ALU.subtract, ALU.mult),
                            waits=[(s_dve, s_dve.n), (s_act, s_act.n), (s_pe, s_pe.n)],
                            inc=s_dve)
                        transpose_chunk(xtm.ap(), xlnt, sc * 128, (s_dve, s_dve.n))

                    def ship(g, t=t):
                        with g.If_eq(regs["preg"], 0):
                            for dc in range(NDC):
                                g.dma_start(out=ap_shx[dc, :, t * 512:(t + 1) * 512],
                                            in_=xlnt[:, dc * 512:(dc + 1) * 512]
                                            ).then_inc(s_st.h, 16)
                        with g.Else():
                            for dc in range(NDC):
                                g.dma_start(out=ap_shx[dc, :, OWN + t * 512: OWN + (t + 1) * 512],
                                            in_=xlnt[:, dc * 512:(dc + 1) * 512]
                                            ).then_inc(s_st.h, 16)
                        return g.nop()

                    P.op("gpsimd", ship, waits=[(s_act, s_act.n), (s_pool, 3)],
                         bump=[(s_st, 16 * NDC)])
                P.op("gpsimd", lambda g: g.nop(),
                     waits=[(s_st, s_st.n), (s_y, s_y.n)], inc=s_x1)
            return s_x1


        hT_b = hT.ap()
        qt_sb = hT_b[:, 0:4096]
        kt_sb = hT_b[:, 4096:8192]
        vfm_sb = hT_b[:, 8192:12288]
        vt_sb = hT_b[:, 12288:16384]
        z_sb = hT_b[:, 16384:20480]
        xs_sb = [hT_b[:, 20480:28672], hT_b[:, 28672:36864]]
        scb = [satt.ap(), xlnt.ap().bitcast(F32)]
        p4buf = [xch.ap().bitcast(BF16), hT_b[:, 20480:36864]]
        pbb_stage = [pb[4].ap().bitcast(BF16), pb[5].ap().bitcast(BF16)]
        awf2 = [wf32.ap()[:, 0:2048], wf32.ap()[:, 2048:4096]]
        aw_b = [wbf[0].ap()[:, 0:2048], wbf[1].ap()[:, 0:2048], wbf[2].ap()[:, 0:2048]]
        pt4 = [wbf[3].ap()[:, 0:1024], wbf[3].ap()[:, 1024:2048]]
        mt = [sq.ap()[:, 0:512], sq.ap()[:, 512:1024],
              sqd.ap()[:, 0:512], sqd.ap()[:, 512:1024]]
        sat_b = satt.ap().bitcast(BF16)
        pbb4 = pb[4].ap().bitcast(BF16)
        pbb3 = pb[3].ap().bitcast(BF16)

        def barrier(tag):
            allw = [(s, s.n) for s in (s_in, s_w, s_cvt, s_pe, s_act, s_dve,
                                       s_pool, s_st, s_cache, s_ld2, s_y)]
            for eng in ("sync", "gpsimd", "scalar", "vector", "tensor"):
                P.op(eng, lambda e: e.nop(), waits=allw)

        def flag_spin(sync_id):
            r0 = sync_id * 2

            def fn(g):
                with g.If_eq(regs["preg"], 0):
                    g.dma_start(out=ap_shf[r0:r0 + 1, :], in_=sb_magic[:, :]
                                ).then_inc(s_st.h, 16)
                with g.Else():
                    g.dma_start(out=ap_shf[r0 + 1:r0 + 2, :], in_=sb_magic[:, :]
                                ).then_inc(s_st.h, 16)

                def cond():
                    g.reg_alu(regs["rwait"], regs["rwait"], 16, ALU.add)
                    with g.If_eq(regs["preg"], 0):
                        g.dma_start(out=sb_flag[:, :], in_=ap_shf[r0 + 1:r0 + 2, 0:1]
                                    ).then_inc(spoll.h, 16)
                    with g.Else():
                        g.dma_start(out=sb_flag[:, :], in_=ap_shf[r0:r0 + 1, 0:1]
                                    ).then_inc(spoll.h, 16)
                    g.wait_ge(spoll.h, regs["rwait"])
                    g.reg_load(regs["rflag"], sb_flag[0:1, 0:1])
                    g.reg_alu(regs["rcond"], regs["rflag"], regs["rmagic"], ALU.not_equal)
                    return regs["rcond"]

                with g.While(cond):
                    g.nop()
                return g.nop()

            P.op("gpsimd", fn,
                 waits=[(s_st, s_st.n), (s_pool, s_pool.n)], inc=s_x,
                 bump=[(s_st, 16)])

        def emit_attention():
            x_dep = (s_x, s_x.n)
            mtok = []
            for qc4 in range(4):
                t1 = P.op("gpsimd", lambda g, qc4=qc4: g.memset(mt[qc4][:, :], 0.0),
                          waits=[(s_act, s_act.n), (s_dve, s_dve.n), (s_pool, s_pool.n)],
                          inc=s_pool)
                mtok.append(P.op("gpsimd", lambda g, qc4=qc4: g.affine_select(
                    out=mt[qc4][:, :], in_=mt[qc4][:, :], pattern=[[-1, 512]],
                    compare_op=ALU.is_ge, fill=regs["rfill"], base=128 * qc4,
                    channel_multiplier=1), waits=[t1], inc=s_pool))

            aw_free = [None, None, None]
            awf_free = [None, None]
            prev_vt_evac = None
            prev_vtrans_last = None
            prev_qk_last = None
            prev_pv_last = None
            prev_trans_p4 = [None, None]
            unit_exp = {}
            unit_mult = {}
            ug = 0
            bank_mm = {b: None for b in range(8)}
            bank_ev = {b: None for b in range(8)}
            stg_ev = {0: None, 1: None}
            pt_free = [None, None]
            ST = {"zc_prev": None, "last_pv": None, "prev_ship": None}
            cvt0 = (s_cvt, s_cvt.n)

            for h in range(HLOC):
                P.scope = f"attn_h{h}"
                aw_cvt = []
                for i, apw in enumerate((ap_wq, ap_wk, ap_wv)):
                    half = i % 2
                    wl = [w for w in (awf_free[half],) if w is not None]
                    if h == 0:
                        wl.append(cvt0)
                    tl = P.op("scalar", lambda s, apw=apw, h=h, half=half: s.dma_start(
                        out=awf2[half].rearrange("p (dc e) -> p dc e", dc=16),
                        in_=apw[:, :, h * 128:(h + 1) * 128].rearrange("dc p e -> p dc e")),
                        waits=wl, inc=s_ld2, amt=16)
                    wl = [tl]
                    if aw_free[i] is not None:
                        wl.append(aw_free[i])
                    ct = P.op("vector", lambda v, i=i, half=half: v.tensor_copy(
                        out=aw_b[i], in_=awf2[half]), waits=wl, inc=s_dve)
                    aw_cvt.append(ct)
                    awf_free[half] = ct
                slab_mm = {}
                evQ = {}
                evK = {}
                evV = {}
                vt_ev = {}
                vtrans_last = None
                for s5 in range(8):
                    buf = s5 % 2
                    wl = [x_dep]
                    if s5 >= 2:
                        wl.append(slab_mm[s5 - 2])
                    elif prev_trans_p4[1] is not None:
                        wl.append(prev_trans_p4[1])
                    ld = P.op("sync", lambda sy, s5=s5, buf=buf: sy.dma_start(
                        out=xs_sb[buf].rearrange("p (dc s) -> p dc s", dc=16),
                        in_=ap_shx[:, :, s5 * 512:(s5 + 1) * 512].rearrange("dc p s -> p dc s")),
                        waits=wl, inc=s_w, amt=16)
                    tk = None
                    for bi in range(3):
                        for dc in range(NDC):
                            wl = []
                            if dc == 0:
                                wl = [ld, aw_cvt[bi]]
                                if bank_ev[bi] is not None:
                                    wl += bank_ev[bi]
                            tk = P.op("tensor", lambda t_, bi=bi, dc=dc, buf=buf: t_.matmul(
                                pb[bi][:, :], aw_b[bi][:, dc * 128:(dc + 1) * 128],
                                xs_sb[buf][:, dc * 512:(dc + 1) * 512],
                                start=(dc == 0), stop=(dc == NDC - 1)),
                                waits=wl, inc=(s_pe if dc == NDC - 1 else None))
                        bank_mm[bi] = tk
                    slab_mm[s5] = tk
                    for bi, dst, dd in ((0, qt_sb, evQ), (1, kt_sb, evK), (2, vfm_sb, evV)):
                        wl = [bank_mm[bi]]
                        if h > 0 and s5 == 0:
                            wl.append(prev_qk_last if bi < 2 else prev_vtrans_last)
                        if bi == 0:
                            tk = P.op("scalar", lambda s, s5=s5: s.activation(
                                qt_sb[:, s5 * 512:(s5 + 1) * 512], pb[0][:, :], AF.Copy,
                                scale=SCALE), waits=wl, inc=s_act)
                        else:
                            tk = P.op("scalar", lambda s, bi=bi, dst=dst, s5=s5: s.activation(
                                dst[:, s5 * 512:(s5 + 1) * 512], pb[bi][:, :], AF.Copy),
                                waits=wl, inc=s_act)
                        dd[s5] = tk
                        bank_ev[bi] = [tk]
                    grp = s5
                    for k in range(4):
                        wl = []
                        if k == 0:
                            wl = [evV[grp]]
                            if grp > 0:
                                wl.append(vt_ev[grp - 1])
                            elif prev_vt_evac is not None:
                                wl.append(prev_vt_evac)
                        tk = P.op("tensor", lambda t, grp=grp, k=k: t.transpose(
                            pbb3[:, k * 128:(k + 1) * 128],
                            vfm_sb[:, (grp * 4 + k) * 128:(grp * 4 + k + 1) * 128],
                            ident[:, :]), waits=wl, inc=(s_pe if k == 3 else None))
                    vtrans_last = tk
                    wl = [tk]
                    if grp == 0 and prev_pv_last is not None:
                        wl.append(prev_pv_last)
                    vt_ev[grp] = P.op("scalar", lambda s, grp=grp: s.activation(
                        vt_sb[:, grp * 512:(grp + 1) * 512], pbb3[:, 0:512], AF.Copy),
                        waits=wl, inc=s_act)
                aw_free = [bank_mm[0], bank_mm[1], bank_mm[2]]

                trans_last_qt = {}
                zc_tok = {}
                qk_banks = (0, 1, 2, 6)
                qki = 0
                last_qk = None

                class PVctx:
                    def __init__(self, qt):
                        self.qt = qt
                        self.nkc = (qt + 1) * 4
                        self.npair = self.nkc // 2
                        self.qpar = qt % 2
                        self.mults = [unit_mult[ug + qt * 4 + i] for i in range(4)]
                        self.ev = {}
                        self.seq = [("t", 0)]
                        for p in range(1, self.npair):
                            self.seq.append(("t", p))
                            self.seq.append(("m", 2 * (p - 1)))
                            self.seq.append(("m", 2 * (p - 1) + 1))
                        self.seq.append(("m", self.nkc - 2))
                        self.seq.append(("m", self.nkc - 1))
                        self.i = 0

                    def step(self):
                        kind, arg = self.seq[self.i]
                        self.i += 1
                        if kind == "t":
                            self._trans(arg)
                        else:
                            self._mm(arg)
                        return self.i >= len(self.seq)

                    def _trans(self, p):
                        j = p % 2
                        last = None
                        for kk in range(8):
                            kc = 2 * p + kk // 4
                            x4 = kk % 4
                            wl = []
                            if kk == 0:
                                if p == 0:
                                    wl += self.mults
                                if stg_ev[j] is not None:
                                    wl.append(stg_ev[j])
                            last = P.op("tensor", lambda t, kc=kc, x4=x4, kk=kk, j=j,
                                        qp=self.qpar: t.transpose(
                                            pbb_stage[j][:, (kk // 4) * 512 + x4 * 128:
                                                         (kk // 4) * 512 + x4 * 128 + 128],
                                            p4buf[qp][:, x4 * 4096 + kc * 128:
                                                      x4 * 4096 + kc * 128 + 128],
                                            ident[:, :]),
                                        waits=wl, inc=(s_pe if kk == 7 else None))
                        if p == self.npair - 1:
                            trans_last_qt[self.qt] = last
                        wl = [last]
                        if pt_free[j] is not None:
                            wl.append(pt_free[j])
                        e = P.op("vector", lambda v, j=j: v.tensor_copy(
                            out=pt4[j], in_=pbb_stage[j][:, 0:1024]), waits=wl, inc=s_dve)
                        self.ev[p] = e
                        stg_ev[j] = e

                    def _mm(self, kc):
                        p = kc // 2
                        j = p % 2
                        wl = []
                        if kc % 2 == 0:
                            wl.append(self.ev[p])
                        else:
                            self.ev.pop(p, None)
                        if kc == 0 and ST["zc_prev"] is not None:
                            wl.append(ST["zc_prev"])
                        tk = P.op("tensor", lambda t_, kc=kc, j=j, nkc=self.nkc: t_.matmul(
                            pb[7][:, :], vt_sb[:, kc * 128:(kc + 1) * 128],
                            pt4[j][:, (kc % 2) * 512:(kc % 2) * 512 + 512],
                            start=(kc == 0), stop=(kc == nkc - 1)),
                            waits=wl, inc=s_pe)
                        if kc % 2 == 1:
                            pt_free[j] = tk
                        ST["last_pv"] = tk
                        if kc == self.nkc - 1:
                            wl = [tk]
                            if self.qt == 0 and ST["prev_ship"] is not None:
                                wl.append(ST["prev_ship"])
                            zc = P.op("scalar", lambda s, qt=self.qt: s.activation(
                                z_sb[:, qt * 512:(qt + 1) * 512], pb[7][:, :], AF.Copy),
                                waits=wl, inc=s_act)
                            zc_tok[self.qt] = zc
                            ST["zc_prev"] = zc

                pvq = deque()

                def pump(n):
                    for _ in range(n):
                        if pvq:
                            if pvq[0].step():
                                pvq.popleft()

                for qt in range(8):
                    for qc4 in range(4):
                        u = ug + qt * 4 + qc4
                        par = u % 2
                        nk = (qt + 1) * 512
                        qi = qt * 4 + qc4
                        c0 = (u % 4) * 4
                        mx = None
                        ev_last_dve = None
                        ev_last_act = None
                        for kt in range(qt + 1):
                            b = qk_banks[qki % 4]
                            qki += 1
                            wl = []
                            if bank_ev[b] is not None:
                                wl += bank_ev[b]
                            if kt == 0:
                                wl += [evQ[qt], evK[qt]]
                            mmtk = P.op("tensor", lambda t_, qi=qi, kt=kt, b=b: t_.matmul(
                                pb[b][:, :], qt_sb[:, qi * 128:(qi + 1) * 128],
                                kt_sb[:, kt * 512:(kt + 1) * 512], start=True, stop=True),
                                waits=wl, inc=s_pe)
                            bank_mm[b] = mmtk
                            last_qk = mmtk
                            wl = [mmtk]
                            if (u - 2) in unit_exp:
                                wl.append(unit_exp[u - 2])
                            if kt == qt:
                                e = P.op("vector", lambda v, kt=kt, b=b, qc4=qc4, par=par:
                                         v.tensor_tensor(
                                             scb[par][:, kt * 512:(kt + 1) * 512],
                                             pb[b][:, :], mt[qc4][:, :], ALU.add),
                                         waits=wl + [mtok[qc4]], inc=s_dve)
                                ev_last_dve = e
                            else:
                                e = P.op("scalar", lambda s, kt=kt, b=b, par=par: s.activation(
                                    scb[par][:, kt * 512:(kt + 1) * 512], pb[b][:, :],
                                    AF.Copy), waits=wl, inc=s_act)
                                ev_last_act = e
                            if kt == 0:
                                wl = [e]
                                if (u - 4) in unit_mult:
                                    wl.append(unit_mult[u - 4])
                                mx = P.op("vector", lambda v, par=par, c0=c0:
                                          v.tensor_reduce(
                                              lgt[:, c0:c0 + 1],
                                              scb[par][:, 0:512], AX.X, ALU.max,
                                              negate=True), waits=wl, inc=s_dve)
                            bank_ev[b] = [e]
                            pump(2)
                        qpar = qt % 2
                        wl = [mx, ev_last_dve]
                        if ev_last_act is not None:
                            wl.append(ev_last_act)
                        if qt >= 2:
                            wl.append(trans_last_qt[qt - 2])
                        elif prev_trans_p4[qpar] is not None:
                            wl.append(prev_trans_p4[qpar])
                        if qt == 1:
                            wl.append(slab_mm[7])
                        ex = P.op("scalar", lambda s, qpar=qpar, qc4=qc4, nk=nk, c0=c0, par=par:
                                  s.activation(
                                      p4buf[qpar][:, qc4 * 4096: qc4 * 4096 + nk],
                                      scb[par][:, 0:nk], AF.Exp,
                                      bias=lgt[:, c0:c0 + 1],
                                      accum_out=lgt[:, c0 + 1:c0 + 2]),
                                  waits=wl, inc=s_act)
                        unit_exp[u] = ex
                        rc = P.op("vector", lambda v, c0=c0: v.reciprocal(
                            lgt[:, c0 + 2:c0 + 3], lgt[:, c0 + 1:c0 + 2]),
                            waits=[ex], inc=s_dve)
                        ml = P.op("vector", lambda v, qpar=qpar, qc4=qc4, nk=nk, c0=c0:
                                  v.tensor_scalar(
                                      p4buf[qpar][:, qc4 * 4096: qc4 * 4096 + nk],
                                      p4buf[qpar][:, qc4 * 4096: qc4 * 4096 + nk],
                                      lgt[:, c0 + 2:c0 + 3], None, ALU.mult),
                                  waits=[rc], inc=s_dve)
                        unit_mult[u] = ml
                        pump(1)
                    pvq.append(PVctx(qt))
                while pvq:
                    if pvq[0].step():
                        pvq.popleft()

                prev_qk_last = last_qk
                prev_pv_last = ST["last_pv"]
                prev_vtrans_last = vtrans_last
                prev_vt_evac = vt_ev[7]
                prev_trans_p4 = [trans_last_qt[6], trans_last_qt[7]]
                ug += 32

                def shipz(g, h=h):
                    with g.If_eq(regs["preg"], 0):
                        g.dma_start(out=ap_shz[h], in_=z_sb).then_inc(s_st.h, 16)
                    with g.Else():
                        g.dma_start(out=ap_shz[HLOC + h], in_=z_sb).then_inc(s_st.h, 16)
                    return g.nop()

                P.op("gpsimd", shipz, waits=[zc_tok[7], (s_pool, s_pool.n)],
                     bump=[(s_st, 16)])
                ST["prev_ship"] = (s_st, s_st.n)

        def emit_outproj():
            x_dep = (s_x, s_x.n)
            wo_bf = hT_b[:, 0:32768]
            zsl = []
            for hh in range(2):
                zsl.append(hT_b[:, 32768 + hh * 2048: 32768 + (hh + 1) * 2048])
            for hh in range(8):
                zsl.append(wbf[hh // 2].ap()[:, (hh % 2) * 2048:(hh % 2 + 1) * 2048])
            for hh in range(4):
                zsl.append(xlnt.ap()[:, hh * 2048:(hh + 1) * 2048])
            sat_b = satt.ap().bitcast(BF16)
            zsl.append(sat_b[:, 0:2048])
            zsl.append(sat_b[:, 2048:4096])
            stg = sat_b[:, 4096:6144]

            cvt_tok = []
            zld_tok = []
            for h in range(16):
                half = h % 2
                wl = [x_dep] if h < 2 else [cvt_tok[h - 2]]
                tl = P.op("scalar", lambda s, h=h, half=half: s.dma_start(
                    out=awf2[half], in_=ap_wo[h]), waits=wl, inc=s_ld2, amt=16)
                ct = P.op("vector", lambda v, h=h, half=half: v.tensor_copy(
                    out=wo_bf[:, h * 2048:(h + 1) * 2048], in_=awf2[half]),
                    waits=[tl], inc=s_dve)
                cvt_tok.append(ct)

                def ldz(g, h=h):
                    with g.If_eq(regs["preg"], 0):
                        g.dma_start(out=zsl[h], in_=ap_shz[h, :, 0:OWN]
                                    ).then_inc(s_in.h, 16)
                    with g.Else():
                        g.dma_start(out=zsl[h], in_=ap_shz[h, :, OWN:SEQ]
                                    ).then_inc(s_in.h, 16)
                    return g.nop()

                P.op("gpsimd", ldz, waits=[x_dep], bump=[(s_in, 16)])
                zld_tok.append((s_in, s_in.n))

            bankf = {0: None, 1: None, 2: None, 3: None}
            xch_free = {}
            nrm_prev = {}
            stg_store = None
            trans_prev = None
            for t in range(NT):
                P.scope = f"outproj_t{t}"
                stv = stats if t % 2 == 0 else lgt
                add_last = {}
                st_toks = []
                for sc4 in range(4):
                    sc = t * 4 + sc4
                    xsl = xch.ap()[:, sc4 * D:(sc4 + 1) * D]
                    wl = xch_free.pop(sc4, None)
                    if wl is None:
                        wl = [(s_dve, s_dve.n), (s_pe, s_pe.n), (s_st, s_st.n)]
                    ld = P.op("gpsimd", lambda g, sc=sc, xsl=xsl: g.dma_start(
                        out=xsl, in_=ap_res1[sc * 128:(sc + 1) * 128, :]),
                        waits=wl, inc=s_in, amt=16)
                    for d5 in range(ND5):
                        bk = d5
                        mmtk = None
                        for h in range(16):
                            wl = []
                            if t == 0 and sc4 == 0 and d5 == 0:
                                wl = [cvt_tok[h], zld_tok[h]]
                            if h == 0 and bankf[bk] is not None:
                                wl = wl + [bankf[bk]]
                            mmtk = P.op("tensor", lambda t_, h=h, sc=sc, d5=d5, bk=bk:
                                        t_.matmul(
                                            pb[bk][:, :],
                                            zsl[h][:, sc * 128:(sc + 1) * 128],
                                            wo_bf[:, h * 2048 + d5 * 512:
                                                  h * 2048 + d5 * 512 + 512],
                                            start=(h == 0), stop=(h == 15)),
                                        waits=wl, inc=(s_pe if h == 15 else None))
                        add = P.op("vector", lambda v, d5=d5, bk=bk, xsl=xsl:
                                   v.tensor_tensor(
                                       xsl[:, d5 * 512:(d5 + 1) * 512], pb[bk][:, :],
                                       xsl[:, d5 * 512:(d5 + 1) * 512], ALU.add),
                                   waits=[mmtk, ld], inc=s_dve)
                        bankf[bk] = add
                    extra = [nrm_prev[t % 2]] if t % 2 in nrm_prev else []
                    dt, at = ln_stats_chunk(xsl, sc4, add, st=stv, extra=extra)
                    st_toks.append((dt, at))
                    add_last[sc4] = add
                rstd = ln_batch_rstd(st=stv)
                for sc4 in range(4):
                    sc = t * 4 + sc4
                    xsl = xch.ap()[:, sc4 * D:(sc4 + 1) * D]
                    st_store = P.op("gpsimd", lambda g, sc=sc, xsl=xsl: g.dma_start(
                        out=ap_res2[sc * 128:(sc + 1) * 128, :], in_=xsl),
                        waits=[add_last[sc4]], inc=s_st, amt=16)
                    wl = [rstd]
                    if trans_prev is not None:
                        wl.append(trans_prev)
                    nrm = P.op("vector", lambda v, sc4=sc4, xsl=xsl, stv=stv:
                               v.tensor_scalar(
                                   xtm[:, :], xsl, stv[:, 16 + sc4:17 + sc4],
                                   stv[:, 32 + sc4:33 + sc4], ALU.subtract, ALU.mult),
                               waits=wl, inc=s_dve)
                    nrm_prev[t % 2] = nrm
                    ev_toks = []
                    for grp in range(4):
                        for k in range(4):
                            dc = grp * 4 + k
                            wl = []
                            if k == 0:
                                wl = [nrm]
                                if ev_toks:
                                    wl.append(ev_toks[-1])
                                elif stg_store is not None:
                                    wl.append(stg_store)
                            tk = P.op("tensor", lambda t_, dc=dc, k=k: t_.transpose(
                                pbb4[:, k * 128:(k + 1) * 128],
                                xtm.ap()[:, dc * 128:(dc + 1) * 128], ident[:, :]),
                                waits=wl, inc=(s_pe if k == 3 else None))
                        wl = [tk]
                        if grp == 0 and stg_store is not None:
                            wl.append(stg_store)
                        ev_toks.append(P.op("scalar", lambda s, grp=grp: s.activation(
                            stg[:, grp * 512:(grp + 1) * 512], pbb4[:, 0:512], AF.Copy),
                            waits=wl, inc=s_act))
                    trans_prev = tk
                    stg_store = P.op("gpsimd", lambda g, sc=sc: g.dma_start(
                        out=ap_xln2t[:, :, sc * 128:(sc + 1) * 128]
                        .rearrange("dc p s -> p dc s"),
                        in_=stg.rearrange("p (dc s) -> p dc s", dc=16)),
                        waits=[ev_toks[-1]], inc=s_st, amt=16)
                    xch_free[sc4] = [st_store, rstd, nrm]

        emit_ffn(1)
        if build == "full":
            P.scope = "spin1"
            flag_spin(0)
            barrier("p2")
            emit_attention()
            P.scope = "spin2"
            flag_spin(1)
            barrier("p3")
            emit_outproj()
            barrier("p4")
            emit_ffn(2)

        if debug_out == "cache":
            P.op("scalar", lambda s: s.dma_start(
                out=ap_y[1024:1152, 0:2048],
                in_=ap_cdown[0, 0].bitcast(F32)),
                waits=[(s_cache, s_cache.n), (s_st, s_st.n)], inc=s_y, amt=16)
            P.op("scalar", lambda s: s.dma_start(
                out=ap_y[1152:1280, 0:2048],
                in_=ap_cgate[0].bitcast(F32)),
                waits=[(s_cache, s_cache.n)], inc=s_y, amt=16)

        def final_wait(g):
            return g.nop()

        P.op("gpsimd", final_wait,
             waits=[(s_y, s_y.n), (s_st, s_st.n), (s_cache, s_cache.n)])
        P.emit()
    return nc



def _layout_inputs(L0_attn_x, L0_attn_W_Q, L0_attn_W_K, L0_attn_W_V, L0_attn_W_O,
                   L0_ffn_router_weight, L0_ffn_sh_gate, L0_ffn_sh_up, L0_ffn_sh_down,
                   L0_ffn_r_gate, L0_ffn_r_up, L0_ffn_r_down):
    c = np.ascontiguousarray
    f = np.asarray
    gate_t = c(f(L0_ffn_sh_gate).T.reshape(16, 128, 32, 256).transpose(2, 0, 1, 3))
    up_t = c(f(L0_ffn_sh_up).T.reshape(16, 128, 32, 256).transpose(2, 0, 1, 3))
    down_t = c(f(L0_ffn_sh_down).T.reshape(64, 128, 4, 512).transpose(2, 0, 1, 3))
    rgate_t = c(f(L0_ffn_r_gate).T.reshape(16, 128, 4, 256).transpose(2, 0, 1, 3))
    rup_t = c(f(L0_ffn_r_up).T.reshape(16, 128, 4, 256).transpose(2, 0, 1, 3))
    rdown_t = c(f(L0_ffn_r_down).T.reshape(8, 128, 4, 512).transpose(2, 0, 1, 3))
    wr_t = c(f(L0_ffn_router_weight).T.reshape(16, 128, 8))
    wo_t = c(f(L0_attn_W_O).transpose(0, 2, 1))
    wq, wk, wv = f(L0_attn_W_Q), f(L0_attn_W_K), f(L0_attn_W_V)
    magic = np.full((1, 64), np.random.randint(1, 2**31 - 1), dtype=np.uint32)
    x = f(L0_attn_x)
    in_maps = []
    for core in range(8):
        b, j = core // 2, core % 2
        hs = slice(8 * j, 8 * j + 8)
        in_maps.append({
            "x": c(x[b, j * 2048:(j + 1) * 2048, :]),
            "gate_t": gate_t, "up_t": up_t, "down_t": down_t,
            "rgate_t": rgate_t, "rup_t": rup_t, "rdown_t": rdown_t,
            "wr_t": wr_t,
            "wq_t": c(wq[hs].transpose(2, 0, 1).reshape(16, 128, 8 * 128)),
            "wk_t": c(wk[hs].transpose(2, 0, 1).reshape(16, 128, 8 * 128)),
            "wv_t": c(wv[hs].transpose(2, 0, 1).reshape(16, 128, 8 * 128)),
            "wo_t": wo_t,
            "magic": magic,
        })
    return in_maps


def kernel(**inputs):
    in_maps = _layout_inputs(**inputs)
    nc = build_program()
    res = run_bass_kernel_spmd(nc, in_maps, core_ids=list(range(8)))
    out = np.empty((4, 4096, 2048), dtype=np.float32)
    for core in range(8):
        b, j = core // 2, core % 2
        out[b, j * 2048:(j + 1) * 2048, :] = res.results[core]["y"]
    return out
